# version 15
# baseline (speedup 1.0000x reference)
import numpy as np
import ml_dtypes
from contextlib import ExitStack
import concourse.bass as bass
import concourse.mybir as mybir
from concourse.bass_utils import run_bass_kernel_spmd

F32 = mybir.dt.float32
BF16 = mybir.dt.bfloat16
I32 = mybir.dt.int32
U8 = mybir.dt.uint8
AF = mybir.ActivationFunctionType
ALU = mybir.AluOpType

T = 2048
D = 2048
NJT = 34
JPROJ = 4256
H = 32
Q = 128
NCH = 16
NTT = 16
NKT = 16
NIT = 64
NTC = 4


class Sem:
    def __init__(self, stack, nc, name):
        self.h = stack.enter_context(nc.semaphore(name))
        self.n = 0

    def inc(self, instr, k=1):
        instr.then_inc(self.h, k)
        self.n += k
        return self.n


def _carve(nc, arena, name, shape, dtype, off):
    assert off % 64 == 0
    hnd = nc.virtual_sbuf_tensor(name, shape, dtype)
    nc._tensor(hnd.name, hnd.shape, hnd.dtype, type="SB")
    mloc = nc.lookup_mloc(hnd)
    mloc.allocated = not nc.target_bir_lowering
    mloc.addr = nc.lookup_mloc(arena).addr + off
    nbytes = int(np.prod(shape[1:])) * mybir.dt.size(dtype)
    assert off + nbytes <= arena.shape[1], (name, off, nbytes, arena.shape)
    return hnd


def build(phases=("A", "B", "D", "E", "F", "G"), taps=()):
    nc = bass.Bass("TRN2", target_bir_lowering=False, debug=False)

    u = nc.dram_tensor("u", [T, D], F32, kind="ExternalInput").ap()
    win = nc.dram_tensor("win", [NJT, 128, NKT * 128], F32, kind="ExternalInput").ap()
    wout = nc.dram_tensor("wout", [NKT, 128, D], F32, kind="ExternalInput").ap()
    wg = nc.dram_tensor("wg", [NIT, 128, NKT * 128], I32, kind="ExternalInput").ap()
    wu = nc.dram_tensor("wu", [NIT, 128, NKT * 128], I32, kind="ExternalInput").ap()
    wd = nc.dram_tensor("wd", [4, 2, 32, 128, 512], I32, kind="ExternalInput").ap()
    gsc_d = nc.dram_tensor("gsc", [128, NIT], F32, kind="ExternalInput").ap()
    usc_d = nc.dram_tensor("usc", [128, NIT], F32, kind="ExternalInput").ap()
    dsc_d = nc.dram_tensor("dsc", [4, 128, 512], F32, kind="ExternalInput").ap()
    cw_d = nc.dram_tensor("cw", [17, 128, 4], F32, kind="ExternalInput").ap()
    cb_d = nc.dram_tensor("cb", [17, 128, 1], F32, kind="ExternalInput").ap()
    dtb_d = nc.dram_tensor("dtb", [32, 1], F32, kind="ExternalInput").ap()
    alog_d = nc.dram_tensor("alog", [32, 1], F32, kind="ExternalInput").ap()
    ddiag_d = nc.dram_tensor("ddiag", [128, 32], F32, kind="ExternalInput").ap()
    tri_d = nc.dram_tensor("tri", [128, 128], F32, kind="ExternalInput").ap()
    idf_d = nc.dram_tensor("idf", [128, 128], F32, kind="ExternalInput").ap()
    eps_d = nc.dram_tensor("eps", [128, 1], F32, kind="ExternalInput").ap()
    ones_d = nc.dram_tensor("onesd", [128, 128], F32, kind="ExternalInput").ap()
    m128_d = nc.dram_tensor("m128d", [128, 1], F32, kind="ExternalInput").ap()

    y = nc.dram_tensor("y", [T, D], F32, kind="ExternalOutput").ap()

    z_sc = nc.dram_tensor("z_sc", [NCH, D, 128], BF16, kind="Internal").ap()
    h_sc = nc.dram_tensor("h_sc", [NTT, NIT, 128, 128], BF16, kind="Internal").ap()
    yg_sc = nc.dram_tensor("yg_sc", [NCH, D, 128], BF16, kind="Internal").ap()
    s_sc = nc.dram_tensor("s_sc", [NTT, 128, D], BF16, kind="Internal").ap()
    x_sc = nc.dram_tensor("x_sc", [NTT, 128, D], BF16, kind="Internal").ap()

    taps_out = {}
    for tname, shape, dt in taps:
        taps_out[tname] = nc.dram_tensor("tap_" + tname, shape, dt,
                                         kind="ExternalOutput").ap()

    stack = ExitStack()
    with stack:
        sbuf = lambda name, shape, dt: stack.enter_context(nc.sbuf_tensor(name, shape, dt))
        sem = lambda name: Sem(stack, nc, name)

        ar1 = sbuf("arena1", [128, 65536], U8)
        ar2 = sbuf("arena2", [128, 65536], U8)
        ar3 = sbuf("arena3", [128, 28672], U8)
        cv = lambda a, n, s, d, o: _carve(nc, a, n, s, d, o)

        BT_bf = sbuf("BT_bf", [64, T], BF16)
        CT_bf = sbuf("CT_bf", [64, T], BF16)
        dtraw = sbuf("dtraw", [32, T], F32)
        identb = sbuf("identb", [128, 128], BF16)
        identf = sbuf("identf", [128, 128], F32)
        tri = sbuf("tri_s", [128, 128], F32)
        onesf = sbuf("onesf", [128, 128], F32)
        gsc = sbuf("gsc_s", [128, NIT], F32)
        ngsc = sbuf("ngsc_s", [128, NIT], F32)
        usc = sbuf("usc_s", [128, NIT], F32)
        cwt = sbuf("cwt", [128, 17 * 4], F32)
        cbt = sbuf("cbt", [128, 17], F32)
        dtb = sbuf("dtb_s", [32, 1], F32)
        alog = sbuf("alog_s", [32, 1], F32)
        ddiag = sbuf("ddiag_s", [128, 32], F32)
        epsc = sbuf("epsc", [128, 1], F32)
        m128c = sbuf("m128c", [128, 1], F32)
        stats = sbuf("stats", [128, 24], F32)
        aggr = sbuf("aggr", [128, 2], F32)
        sqv = sbuf("sqv", [128, 1], F32)
        rstd = sbuf("rstd", [128, 1], F32)
        nmr = sbuf("nmr", [128, 1], F32)
        a_tmp = sbuf("a_tmp", [32, 1], F32)
        A_col = sbuf("A_col", [32, 1], F32)

        xnT = cv(ar1, "xnT", [128, NKT * T], BF16, 0)
        u_t = [cv(ar3, f"u_t{i}", [128, T], F32, i * 8192) for i in range(2)]
        xn_t = [cv(ar3, f"xn_t{i}", [128, T], BF16, 16384 + i * 4096) for i in range(2)]
        winb = [cv(ar3, f"winb{i}", [128, NKT * 128], BF16, i * 4096) for i in range(2)]
        padb = [cv(ar3, f"pad{i}", [128, T + 3], BF16, 8192 + i * 4352) for i in range(2)]
        acc = cv(ar3, "acc", [128, T], F32, 16896)
        zst = [cv(ar3, f"zst{i}", [128, 512], BF16, 25088 + i * 1024) for i in range(2)]
        wgb = [cv(ar3, f"wgb{i}", [128, NKT * 128], BF16, i * 4096) for i in range(2)]
        wub = [cv(ar3, f"wub{i}", [128, NKT * 128], BF16, 8192 + i * 4096) for i in range(2)]
        sgb = [cv(ar3, f"sgb{i}", [128, 512], BF16, 16384 + i * 1024) for i in range(2)]
        hst = [cv(ar3, f"hst{i}", [128, 512], BF16, 18432 + i * 1024) for i in range(2)]
        xch = [cv(ar3, f"xch{i}", [128, D], BF16, 20480 + i * 4096) for i in range(2)]
        dtT = cv(ar3, "dtT", [32, T], F32, 0)
        alphaT = cv(ar3, "alphaT", [32, T], F32, 8192)
        spx = cv(ar3, "spx", [32, T], F32, 16384)
        lamT_all = cv(ar3, "lamT_all", [32, T], F32, 16384)
        zsl = [cv(ar2, f"zsl{i}", [128, NKT * 128], BF16, 54784 + i * 4096) for i in range(2)]
        egb = [cv(ar2, f"egb{i}", [128, 512], BF16, 62976 + i * 1024) for i in range(2)]
        ygb = [cv(ar2, f"ygb{i}", [128, NKT * 128], BF16, i * 4096) for i in range(2)]
        S_f = cv(ar2, "S_f", [64, H * 64], F32, 8192)
        S_bf = cv(ar2, "S_bf", [64, H * 64], BF16, 16384)
        zch = [cv(ar2, f"zch{i}", [128, NKT * 128], BF16, 20480 + i * 4096) for i in range(2)]
        Lam_all = cv(ar2, "Lam_all", [128, NCH * 32], F32, 28672)
        negLam_all = cv(ar2, "negLam_all", [128, NCH * 32], F32, 30720)
        dt_tm_all = cv(ar2, "dt_tm_all", [128, NCH * 32], F32, 32768)
        W_all_all = cv(ar2, "W_all_all", [128, NCH * 32], F32, 34816)
        B_tm_all = cv(ar2, "B_tm_all", [128, NCH * 64], BF16, 36864)
        GT_all = cv(ar2, "GT_all", [128, NCH * 128], BF16, 38912)
        eoff = [43008]
        def ecv(name, shape, dt, n=1):
            outs = []
            for i in range(n):
                nb = int(np.prod(shape[1:])) * mybir.dt.size(dt)
                nb = (nb + 63) // 64 * 64
                outs.append(cv(ar2, f"{name}{i}", shape, dt, eoff[0]))
                eoff[0] += nb
            return outs
        V_Tp = ecv("V_Tp", [32, 128], F32, 2)
        W_allTp = ecv("W_allTp", [32, 128], F32, 2)
        D1 = ecv("D1", [128, 128], F32, 4)
        E_sb = ecv("E_sb", [128, 128], F32, 4)
        t1 = ecv("t1", [128, 128], F32, 2)
        Mf = ecv("Mf", [128, 128], BF16, 4)
        Cd = ecv("Cd", [64, 128], BF16, 4)
        Bd = ecv("Bd", [128, 64], BF16, 4)
        Dinb = ecv("Dinb", [64, 128], F32, 4)
        assert eoff[0] <= 54784
        WoutT = cv(ar1, "WoutT", [128, NKT * T], BF16, 0)
        ygp = [cv(ar3, f"ygp{i}", [128, NKT * 128], BF16, i * 4096) for i in range(2)]
        s_blk = [cv(ar3, f"s_blk{i}", [128, 512], BF16, 8192 + i * 1024) for i in range(2)]
        wdp = [cv(ar1, f"wdp{i}", [128, 32 * 512], BF16, i * 32768) for i in range(2)]
        h_tt = [cv(ar2, f"h_tt{i}", [128, 32 * 128], BF16, i * 8192) for i in range(2)]
        dscst = [cv(ar2, f"dscst{i}", [128, 512], F32, 16384 + i * 2048) for i in range(2)]
        sstrip = [cv(ar2, f"sstrip{i}", [128, 512], BF16, 20480 + i * 1024) for i in range(2)]
        ustrip = [cv(ar2, f"ustrip{i}", [128, 512], F32, 22528 + i * 2048) for i in range(2)]
        gsil = [cv(ar2, f"gsil{i}", [128, 512], F32, 26624 + i * 2048) for i in range(2)]
        outb = [cv(ar2, f"outb{i}", [128, 512], F32, 30720 + i * 2048) for i in range(2)]
        scld = [cv(ar2, f"scld{i}", [128, 512], F32, 34816 + i * 2048) for i in range(2)]

        c_ld = sem("c_ld")

        with nc.Block() as blk:
            @blk.gpsimd
            def _(g):
                g.dma_start(out=identb[:], in_=idf_d).then_inc(c_ld.h, 16)
                c_ld.n += 16
            @blk.sync
            def _(s):
                for dst, src in ((identf, idf_d), (tri, tri_d), (onesf, ones_d),
                                 (gsc, gsc_d), (usc, usc_d),
                                 (dtb, dtb_d), (alog, alog_d), (ddiag, ddiag_d),
                                 (epsc, eps_d), (m128c, m128_d)):
                    s.dma_start(out=dst[:], in_=src).then_inc(c_ld.h, 16)
                    c_ld.n += 16
                s.dma_start(out=cwt.ap().rearrange("p (c k) -> p c k", c=17),
                            in_=cw_d.rearrange("c p k -> p c k")).then_inc(c_ld.h, 16)
                c_ld.n += 16
                s.dma_start(out=cbt.ap().rearrange("p (c o) -> p c o", c=17),
                            in_=cb_d.rearrange("c p o -> p c o"),
                            allow_slow_non_contiguous=True).then_inc(c_ld.h, 16)
                c_ld.n += 16
        CONST = c_ld.n

        a_ld = [sem("a_ld0"), sem("a_ld1")]; a_stats = sem("a_st"); a_ag = sem("a_ag")
        a_xn = sem("a_xn"); a_tr = sem("a_tr"); a_ev = sem("a_ev")
        b_wld = [sem("b_wld0"), sem("b_wld1")]; b_mm = sem("b_mm"); b_ev = sem("b_ev")
        b_zst = [sem("b_zst0"), sem("b_zst1")]; b_cv = sem("b_cv"); b_sil = sem("b_sil")
        b_xtr = sem("b_xtr"); b_xev = sem("b_xev"); b_bc = sem("b_bc")
        wgld = [sem("wgld0"), sem("wgld1")]; wuld = [sem("wuld0"), sem("wuld1")]
        d_cvtg = sem("d_cvtg"); d_cvtu = sem("d_cvtu"); d_mm = sem("d_mm")
        d_sgd = sem("d_sgd"); e_ng = sem("e_ng")
        d_sg = sem("d_sg"); d_h = sem("d_h"); d_hst = [sem("d_hst0"), sem("d_hst1")]
        e_pp = sem("e_pp"); e_scan = sem("e_scan"); e_pt = sem("e_pt")
        e_ptev = sem("e_ptev"); e_lev = sem("e_lev"); e_vt = sem("e_vt")
        e_wt = sem("e_wt"); e_btr = sem("e_btr"); e_bev = sem("e_bev")
        e_gmm = sem("e_gmm"); e_gt = sem("e_gt"); e_zld = [sem("e_zld0"), sem("e_zld1")]
        e_ygst = [sem("e_ygst0"), sem("e_ygst1")]; e_bcx = sem("e_bcx"); e_d1 = sem("e_d1")
        e_exp = sem("e_exp"); e_din = sem("e_din"); e_mf = sem("e_mf")
        e_cd = sem("e_cd"); e_bd = sem("e_bd"); e_ymm = sem("e_ymm")
        e_smm = sem("e_smm"); e_sn = sem("e_sn"); e_sbf = sem("e_sbf")
        e_zs = sem("e_zs"); e_yg = sem("e_yg")
        e_xld = [sem("e_xld0"), sem("e_xld1")]
        b_xsv = sem("b_xsv"); b_xst = [sem("b_xst0"), sem("b_xst1")]
        f_wld = sem("f_wld"); f_yld = [sem("f_yld0"), sem("f_yld1")]; f_mm = sem("f_mm")
        f_sev = sem("f_sev"); f_sst = [sem("f_sst0"), sem("f_sst1")]
        g_dsc = [sem("g_dsc0"), sem("g_dsc1")]; g_wld = [sem("g_wld0"), sem("g_wld1")]
        g_cvt = sem("g_cvt")
        g_hld = [sem("g_hld0"), sem("g_hld1")]; g_mm = sem("g_mm"); g_sil = sem("g_sil")
        g_sld = [sem("g_sld0"), sem("g_sld1")]; g_uld = [sem("g_uld0"), sem("g_uld1")]
        g_ob = sem("g_ob")
        g_scl = sem("g_scl")
        g_ost = [sem("g_ost0"), sem("g_ost1")]

        if "A" in phases:
            with ExitStack() as pstack:
                psum = lambda name, shape, dt: pstack.enter_context(nc.psum_tensor(name, shape, dt))
                ps_tr = [psum(f"ps_tr{i}", [128, 128], BF16) for i in range(2)]
                with nc.Block() as blk:
                    @blk.sync
                    def _(s):
                        for tt in range(NTT):
                            if tt >= 2:
                                s.wait_ge(a_xn.h, tt - 1)
                            s.dma_start(out=u_t[tt % 2][:],
                                        in_=u[tt * 128:(tt + 1) * 128, :]
                                        ).then_inc(a_ld[tt % 2].h, 16)
                            a_ld[tt % 2].n += 16

                    @blk.vector
                    def _(v):
                        def evacs(tt):
                            for kt in range(NKT):
                                g = tt * NKT + kt
                                v.wait_ge(a_tr.h, g + 1)
                                ins = v.tensor_copy(
                                    out=xnT[:, kt * T + tt * 128: kt * T + (tt + 1) * 128],
                                    in_=ps_tr[g % 2][:])
                                a_ev.inc(ins)
                        for tt in range(NTT):
                            v.wait_ge(a_ld[tt % 2].h, 16 * (tt // 2 + 1))
                            for qq in range(4):
                                ins = v.bn_stats(out=stats[:, 6 * qq:6 * qq + 6],
                                                 in_=u_t[tt % 2][:, 512 * qq:512 * (qq + 1)])
                            a_stats.inc(ins)
                            v.wait_ge(a_stats.h, tt + 1)
                            ins = v.bn_aggr(out=aggr[:], in_=stats[:])
                            a_ag.inc(ins)
                            v.wait_ge(a_ag.h, 4 * tt + 2)
                            ins = v.reciprocal(out=rstd[:], in_=sqv[:])
                            a_ag.inc(ins)
                            v.wait_ge(a_ag.h, 4 * tt + 3)
                            ins = v.tensor_scalar(out=nmr[:], in0=aggr[:, 0:1],
                                                  scalar1=rstd[:], scalar2=-1.0,
                                                  op0=ALU.mult, op1=ALU.mult)
                            a_ag.inc(ins)
                            if tt >= 1:
                                evacs(tt - 1)
                        evacs(NTT - 1)

                    @blk.scalar
                    def _(sc):
                        for tt in range(NTT):
                            sc.wait_ge(a_ag.h, 4 * tt + 1)
                            ins = sc.activation(out=sqv[:], in_=aggr[:, 1:2],
                                                func=AF.Sqrt, bias=epsc[:])
                            a_ag.inc(ins)
                            sc.wait_ge(a_ag.h, 4 * tt + 4)
                            if tt >= 2:
                                sc.wait_ge(a_tr.h, 16 * (tt - 1))
                            ins = sc.activation(out=xn_t[tt % 2][:], in_=u_t[tt % 2][:],
                                                func=AF.Identity, bias=nmr[:], scale=rstd[:])
                            a_xn.inc(ins)

                    @blk.tensor
                    def _(t):
                        t.wait_ge(c_ld.h, CONST)
                        for tt in range(NTT):
                            t.wait_ge(a_xn.h, tt + 1)
                            for kt in range(NKT):
                                g = tt * NKT + kt
                                if g >= 2:
                                    t.wait_ge(a_ev.h, g - 1)
                                ins = t.transpose(ps_tr[g % 2][:],
                                                  xn_t[tt % 2][:, kt * 128:(kt + 1) * 128],
                                                  identb[:])
                                a_tr.inc(ins)

        XT = list(range(16, 32))
        if "B" in phases:
            with ExitStack() as pstack:
                psum = lambda name, shape, dt: pstack.enter_context(nc.psum_tensor(name, shape, dt))
                ps_pj = [psum(f"ps_pj{i}", [128, 512], F32) for i in range(2)]
                ps_xtr = [psum(f"ps_xtr{i}", [128, 128], F32) for i in range(2)]
                with nc.Block() as blk:
                    @blk.gpsimd
                    def _(g):
                        if "A" in phases:
                            g.wait_ge(a_xn.h, NTT)
                            g.wait_ge(a_tr.h, 256)
                        for jt in range(NJT):
                            if jt >= 2:
                                g.wait_ge(b_mm.h, 4 * jt - 4)
                            g.dma_start(out=winb[jt % 2][:], in_=win[jt]
                                        ).then_inc(b_wld[jt % 2].h, 16)
                            b_wld[jt % 2].n += 16
                        g.wait_ge(b_sil.h, 17)
                        g.dma_start(out=CT_bf[:], in_=acc[64:128, :]).then_inc(b_bc.h, 16)
                        b_bc.n += 16

                    @blk.tensor
                    def _(t):
                        t.wait_ge(a_ev.h, a_ev.n)
                        def emit_xtr(jtx):
                            t.wait_ge(b_sil.h, jtx - 15)
                            for tt in range(NTT):
                                gx = (jtx - 16) * NTT + tt
                                if gx >= 2:
                                    t.wait_ge(b_xev.h, gx - 1)
                                ins = t.transpose(ps_xtr[gx % 2][:],
                                                  acc[:, tt * 128:(tt + 1) * 128],
                                                  identf[:])
                                b_xtr.inc(ins)
                        for jt in range(NJT):
                            t.wait_ge(b_wld[jt % 2].h, 16 * (jt // 2 + 1))
                            for tc in range(NTC):
                                idx = 4 * jt + tc
                                if idx >= 2:
                                    t.wait_ge(b_ev.h, idx - 1)
                                for k in range(NKT):
                                    ins = t.matmul(
                                        ps_pj[idx % 2][:],
                                        winb[jt % 2][:, k * 128:(k + 1) * 128],
                                        xnT[:, k * T + tc * 512: k * T + tc * 512 + 512],
                                        start=(k == 0), stop=(k == NKT - 1))
                                b_mm.inc(ins)
                            if jt - 1 in XT:
                                emit_xtr(jt - 1)

                    @blk.vector
                    def _(v):
                        v.memset(padb[0][:, 0:3], 0.0)
                        v.memset(padb[1][:, 0:3], 0.0)
                        for jt in range(NJT):
                            for tc in range(NTC):
                                idx = 4 * jt + tc
                                v.wait_ge(b_mm.h, idx + 1)
                                if jt <= 15:
                                    if idx >= 2:
                                        v.wait_ge(b_zst[idx % 2].h, 64 * (idx // 2))
                                    ins = v.tensor_copy(out=zst[idx % 2][:],
                                                        in_=ps_pj[idx % 2][:])
                                elif jt <= 32:
                                    ins = v.tensor_copy(
                                        out=padb[jt % 2][:, 3 + tc * 512: 3 + tc * 512 + 512],
                                        in_=ps_pj[idx % 2][:])
                                else:
                                    ins = v.tensor_copy(
                                        out=dtraw[0:32, tc * 512:(tc + 1) * 512],
                                        in_=ps_pj[idx % 2][0:32, :])
                                b_ev.inc(ins)
                            if jt - 1 in XT:
                                if jt - 1 == 17:
                                    v.wait_ge(b_zst[0].h, 2048)
                                    v.wait_ge(b_zst[1].h, 2048)
                                for tt in range(NTT):
                                    gx = (jt - 17) * NTT + tt
                                    grp, q = gx // 4, gx % 4
                                    v.wait_ge(b_xtr.h, gx + 1)
                                    if q == 0 and grp >= 2:
                                        v.wait_ge(b_xst[grp % 2].h, 16 * (grp // 2))
                                    ins = v.tensor_copy(
                                        out=zst[grp % 2][:, q * 128:(q + 1) * 128],
                                        in_=ps_xtr[gx % 2][:])
                                    b_xev.inc(ins)
                            if 16 <= jt <= 32:
                                ct = jt - 16
                                if jt >= 17 and (jt - 1) in XT:
                                    v.wait_ge(b_xtr.h, 16 * (jt - 16))
                                pb = padb[jt % 2]
                                v.tensor_scalar(out=acc[:], in0=pb[:, 0:T],
                                                scalar1=cwt[:, ct * 4: ct * 4 + 1],
                                                scalar2=None, op0=ALU.mult)
                                for k in range(1, 4):
                                    ins = v.scalar_tensor_tensor(
                                        out=acc[:], in0=pb[:, k:k + T],
                                        scalar=cwt[:, ct * 4 + k: ct * 4 + k + 1],
                                        in1=acc[:], op0=ALU.mult, op1=ALU.add)
                                b_cv.inc(ins)
                        v.wait_ge(b_sil.h, 17)
                        ins = v.tensor_copy(out=BT_bf[:], in_=acc[0:64, :])
                        b_bc.inc(ins)

                    @blk.scalar
                    def _(sc):
                        for jtx in range(16, 33):
                            ct = jtx - 16
                            sc.wait_ge(b_cv.h, ct + 1)
                            ins = sc.activation(out=acc[:], in_=acc[:], func=AF.Silu,
                                                bias=cbt[:, ct:ct + 1])
                            b_sil.inc(ins)

                    @blk.sync
                    def _(s):
                        for jt in range(16):
                            for tc in range(NTC):
                                idx = 4 * jt + tc
                                s.wait_ge(b_ev.h, idx + 1)
                                for q in range(4):
                                    s.dma_start(
                                        out=z_sc[tc * 4 + q, jt * 128:(jt + 1) * 128, :],
                                        in_=zst[idx % 2][:, q * 128:(q + 1) * 128]
                                    ).then_inc(b_zst[idx % 2].h, 16)
                                    b_zst[idx % 2].n += 16
                        for jtx in XT:
                            col = jtx - 16
                            for tg in range(4):
                                gx0 = col * NTT + tg * 4
                                grp = gx0 // 4
                                s.wait_ge(b_xev.h, gx0 + 4)
                                s.dma_start(
                                    out=x_sc[tg * 4:tg * 4 + 4].rearrange(
                                        "a p t -> p a t")[:, :, col * 128:(col + 1) * 128],
                                    in_=zst[grp % 2][:].rearrange("p (a t) -> p a t", a=4)
                                ).then_inc(b_xst[grp % 2].h, 16)
                                b_xst[grp % 2].n += 16

        if "E" in phases:
            with ExitStack() as pstack:
                psum = lambda name, shape, dt: pstack.enter_context(nc.psum_tensor(name, shape, dt))
                ps_e = [psum(f"ps_e{i}", [128, 128], F32) for i in range(3)]
                ps_y = [psum(f"ps_y{i}", [128, 128], F32) for i in range(2)]
                ps_s = psum("ps_s", [64, 64], F32)
                ps_g = psum("ps_g", [128, 512], F32)
                ps_u = psum("ps_u", [128, 512], F32)
                ps_btm = ps_y[0][:].bitcast(BF16)[:, 0:64]
                BEV = b_ev.n
                BZST0 = b_zst[0].n; BZST1 = b_zst[1].n
                BXST0 = b_xst[0].n; BXST1 = b_xst[1].n
                NG = 32 * NCH
                DO_D = "D" in phases

                def dgrp(gidx):
                    return gidx // 8, (gidx % 8) // 2, gidx % 2

                with nc.Block() as blk:
                    @blk.vector
                    def _(v):
                        if "B" in phases:
                            v.wait_ge(b_ev.h, BEV)
                            v.wait_ge(b_bc.h, 17)
                        v.memset(S_f[:], 0.0)
                        ins = v.tensor_scalar(out=spx[:], in0=dtraw[:],
                                              scalar1=dtb[:], scalar2=None, op0=ALU.add)
                        e_pp.inc(ins)
                        v.wait_ge(e_pp.h, 3)
                        ins = v.tensor_tensor(out=dtT[:], in0=dtT[:], in1=alphaT[:],
                                              op=ALU.add)
                        e_pp.inc(ins)
                        ins = v.tensor_single_scalar(out=A_col[:], in_=a_tmp[:],
                                                     scalar=-1.0, op=ALU.mult)
                        e_pp.inc(ins)
                        v.wait_ge(e_pp.h, 5)
                        ins = v.tensor_scalar(out=alphaT[:], in0=dtT[:],
                                              scalar1=A_col[:], scalar2=None, op0=ALU.mult)
                        e_pp.inc(ins)
                        for c in range(NCH):
                            ins = v.tensor_tensor_scan(
                                out=lamT_all[:, c * 128:(c + 1) * 128],
                                data0=onesf[0:32, 0:128],
                                data1=alphaT[:, c * 128:(c + 1) * 128],
                                initial=0.0, op0=ALU.mult, op1=ALU.add)
                            e_scan.inc(ins)
                        for c in range(NCH):
                            v.wait_ge(e_vt.h, c + 1)
                            ins = v.tensor_tensor(out=W_allTp[c % 2][:],
                                                  in0=dtT[:, c * 128:(c + 1) * 128],
                                                  in1=V_Tp[c % 2][:], op=ALU.mult)
                            e_wt.inc(ins)
                            for k in range(4):
                                kk = 4 * c + k
                                v.wait_ge(e_pt.h, kk + 1)
                                if k == 0:
                                    ins = v.tensor_copy(out=Lam_all[:, c * 32:(c + 1) * 32],
                                                        in_=ps_e[kk % 3][:, 0:32])
                                elif k == 1:
                                    ins = v.tensor_copy(out=dt_tm_all[:, c * 32:(c + 1) * 32],
                                                        in_=ps_e[kk % 3][:, 0:32])
                                elif k == 2:
                                    ins = v.tensor_copy(out=W_all_all[:, c * 32:(c + 1) * 32],
                                                        in_=ps_e[kk % 3][:, 0:32])
                                else:
                                    ins = v.tensor_tensor(
                                        out=GT_all[:, c * 128:(c + 1) * 128],
                                        in0=ps_e[kk % 3][:], in1=tri[:], op=ALU.mult)
                                e_ptev.inc(ins)
                            v.wait_ge(e_btr.h, c + 1)
                            ins = v.tensor_copy(out=B_tm_all[:, c * 64:(c + 1) * 64],
                                                in_=ps_btm)
                            e_bev.inc(ins)
                        ins = v.tensor_single_scalar(out=negLam_all[:], in_=Lam_all[:],
                                                     scalar=-1.0, op=ALU.mult)
                        e_lev.inc(ins)
                        v.wait_ge(e_pt.h, 64)
                        if DO_D:
                            v.wait_ge(c_ld.h, CONST)
                            ins = v.tensor_single_scalar(out=ngsc[:], in_=gsc[:],
                                                         scalar=-1.0, op=ALU.mult)
                            e_ng.inc(ins)
                        def dve_d1(gidx):
                            c, h = gidx // 32, gidx % 32
                            v.wait_ge(e_bcx.h, gidx + 1)
                            if gidx >= 4:
                                v.wait_ge(e_exp.h, gidx - 3)
                            ins = v.scalar_tensor_tensor(
                                out=D1[gidx % 4][:], in0=ps_e[gidx % 3][:],
                                scalar=negLam_all[:, c * 32 + h:c * 32 + h + 1], in1=tri[:],
                                op0=ALU.add, op1=ALU.mult)
                            e_d1.inc(ins)
                        def dve_tail(gidx):
                            c, h = gidx // 32, gidx % 32
                            par = c % 2
                            rows = slice(64 * (h % 2), 64 * (h % 2) + 64)
                            v.wait_ge(e_smm.h, gidx + 1)
                            ins = v.scalar_tensor_tensor(
                                out=S_f[:, 64 * h:64 * h + 64],
                                in0=S_f[:, 64 * h:64 * h + 64],
                                scalar=Dinb[gidx % 4][0:64, 127:128], in1=ps_s[:],
                                op0=ALU.mult, op1=ALU.add)
                            e_sn.inc(ins)
                            v.wait_ge(e_sn.h, gidx + 1)
                            ins = v.tensor_copy(out=S_bf[:, 64 * h:64 * h + 64],
                                                in_=S_f[:, 64 * h:64 * h + 64])
                            e_sbf.inc(ins)
                            v.wait_ge(e_ymm.h, gidx + 1)
                            v.wait_ge(e_zs.h, c + 1)
                            if h == 0 and c >= 2:
                                v.wait_ge(e_ygst[c % 2].h, 16 * (c // 2))
                            ins = v.tensor_tensor(
                                out=ygb[par][rows, (h // 2) * 128:(h // 2) * 128 + 128],
                                in0=ps_y[gidx % 2][rows, :],
                                in1=zsl[par][rows, (h // 2) * 128:(h // 2) * 128 + 128],
                                op=ALU.mult)
                            e_yg.inc(ins)
                        def dve_dops(gidx):
                            if not DO_D:
                                return
                            if gidx % 8 == 6 and gidx // 8 + 1 < NIT:
                                it = gidx // 8 + 1
                                v.wait_ge(wuld[it % 2].h, 16 * (it // 2 + 1))
                                ins = v.tensor_scalar(out=wub[it % 2][:], in0=wub[it % 2][:],
                                                      scalar1=m128c[:], scalar2=None,
                                                      op0=ALU.add)
                                d_cvtu.inc(ins)
                            if gidx >= 2 and gidx % 2 == 0:
                                gg = (gidx - 2) // 2
                                it, tc, _ = dgrp(2 * gg)
                                v.wait_ge(d_mm.h, 2 * gg + 2)
                                v.wait_ge(d_sg.h, gg + 1)
                                if gg >= 2:
                                    v.wait_ge(d_hst[gg % 2].h, 64 * (gg // 2))
                                v.tensor_scalar(out=egb[gg % 2][:], in0=egb[gg % 2][:],
                                                scalar1=1.0, scalar2=None, op0=ALU.add)
                                with nc.allow_low_precision("sigmoid denom in bf16; within tolerance"):
                                    v.reciprocal(out=egb[gg % 2][:], in_=egb[gg % 2][:])
                                ins = v.scalar_tensor_tensor(
                                    out=sgb[gg % 2][:], in0=ps_g[:],
                                    scalar=gsc[:, it:it + 1], in1=egb[gg % 2][:],
                                    op0=ALU.mult, op1=ALU.mult)
                                d_sgd.inc(ins)
                                ins = v.scalar_tensor_tensor(
                                    out=hst[gg % 2][:], in0=ps_u[:],
                                    scalar=usc[:, it:it + 1], in1=sgb[gg % 2][:],
                                    op0=ALU.mult, op1=ALU.mult)
                                d_h.inc(ins)
                        if DO_D:
                            v.wait_ge(wuld[0].h, 16)
                            ins = v.tensor_scalar(out=wub[0][:], in0=wub[0][:],
                                                  scalar1=m128c[:], scalar2=None, op0=ALU.add)
                            d_cvtu.inc(ins)
                        dve_d1(0)
                        for c in range(NCH):
                            for h in range(H):
                                gidx = 32 * c + h
                                ch = c * 32 + h
                                if gidx + 1 < NG:
                                    dve_d1(gidx + 1)
                                dve_dops(gidx)
                                v.wait_ge(e_exp.h, gidx + 1)
                                if gidx >= 4:
                                    v.wait_ge(e_ymm.h, gidx - 3)
                                    v.wait_ge(e_smm.h, gidx - 3)
                                v.scalar_tensor_tensor(
                                    out=t1[gidx % 2][:], in0=E_sb[gidx % 4][:],
                                    scalar=dt_tm_all[:, ch:ch + 1],
                                    in1=GT_all[:, c * 128:(c + 1) * 128],
                                    op0=ALU.mult, op1=ALU.mult)
                                ins = v.scalar_tensor_tensor(
                                    out=Mf[gidx % 4][:], in0=identf[:],
                                    scalar=ddiag[:, h:h + 1], in1=t1[gidx % 2][:],
                                    op0=ALU.mult, op1=ALU.add)
                                e_mf.inc(ins)
                                v.wait_ge(e_din.h, gidx + 1)
                                ins = v.tensor_tensor(out=Cd[gidx % 4][:],
                                                      in0=CT_bf[:, c * 128:(c + 1) * 128],
                                                      in1=Dinb[gidx % 4][:], op=ALU.mult)
                                e_cd.inc(ins)
                                ins = v.tensor_scalar(out=Bd[gidx % 4][:],
                                                      in0=B_tm_all[:, c * 64:(c + 1) * 64],
                                                      scalar1=W_all_all[:, ch:ch + 1],
                                                      scalar2=None, op0=ALU.mult)
                                e_bd.inc(ins)
                                if gidx >= 1:
                                    dve_tail(gidx - 1)
                        dve_tail(NG - 1)
                        if DO_D:
                            gg = (NG - 2) // 2
                            it, tc, _ = dgrp(2 * gg)
                            v.wait_ge(d_mm.h, 2 * gg + 2)
                            v.wait_ge(d_sg.h, gg + 1)
                            v.wait_ge(d_hst[gg % 2].h, 64 * (gg // 2))
                            v.tensor_scalar(out=egb[gg % 2][:], in0=egb[gg % 2][:],
                                            scalar1=1.0, scalar2=None, op0=ALU.add)
                            with nc.allow_low_precision("sigmoid denom in bf16; within tolerance"):
                                v.reciprocal(out=egb[gg % 2][:], in_=egb[gg % 2][:])
                            ins = v.scalar_tensor_tensor(
                                out=sgb[gg % 2][:], in0=ps_g[:],
                                scalar=gsc[:, it:it + 1], in1=egb[gg % 2][:],
                                op0=ALU.mult, op1=ALU.mult)
                            d_sgd.inc(ins)
                            ins = v.scalar_tensor_tensor(
                                out=hst[gg % 2][:], in0=ps_u[:],
                                scalar=usc[:, it:it + 1], in1=sgb[gg % 2][:],
                                op0=ALU.mult, op1=ALU.mult)
                            d_h.inc(ins)

                    @blk.scalar
                    def _(sc):
                        if "B" in phases:
                            sc.wait_ge(b_ev.h, BEV)
                        sc.wait_ge(e_pp.h, 1)
                        sc.activation(out=alphaT[:], in_=spx[:], func=AF.Abs)
                        sc.activation(out=alphaT[:], in_=alphaT[:], func=AF.Exp, scale=-1.0)
                        sc.activation(out=alphaT[:], in_=alphaT[:], func=AF.Ln, bias=1.0)
                        ins = sc.activation(out=dtT[:], in_=spx[:], func=AF.Relu)
                        e_pp.inc(ins)
                        ins = sc.activation(out=a_tmp[:], in_=alog[:], func=AF.Exp)
                        e_pp.inc(ins)
                        for c in range(NCH):
                            sc.wait_ge(e_scan.h, c + 1)
                            if c >= 2:
                                sc.wait_ge(e_wt.h, c - 1)
                            ins = sc.activation(out=V_Tp[c % 2][:],
                                                in_=lamT_all[:, c * 128:(c + 1) * 128],
                                                func=AF.Exp, scale=-1.0,
                                                bias=lamT_all[:, c * 128 + 127:c * 128 + 128])
                            e_vt.inc(ins)
                        sc.wait_ge(e_pt.h, 64)
                        sc.wait_ge(e_wt.h, NCH)
                        if DO_D:
                            sc.wait_ge(e_ng.h, 1)
                        def act_dops(gidx):
                            if not DO_D:
                                return
                            if gidx % 8 == 2 and gidx // 8 + 1 < NIT:
                                it = gidx // 8 + 1
                                sc.wait_ge(wgld[it % 2].h, 16 * (it // 2 + 1))
                                ins = sc.activation(out=wgb[it % 2][:], in_=wgb[it % 2][:],
                                                    func=AF.Identity, bias=m128c[:])
                                d_cvtg.inc(ins)
                            if gidx >= 1 and (gidx - 1) % 2 == 0:
                                gg = (gidx - 1) // 2
                                it, tc, _ = dgrp(2 * gg)
                                sc.wait_ge(d_mm.h, 2 * gg + 1)
                                if gg >= 2:
                                    sc.wait_ge(d_sgd.h, gg - 1)
                                ins = sc.activation(out=egb[gg % 2][:],
                                                    in_=ps_g[:],
                                                    func=AF.Exp,
                                                    scale=ngsc[:, it:it + 1])
                                d_sg.inc(ins)
                        if DO_D:
                            sc.wait_ge(wgld[0].h, 16)
                            ins = sc.activation(out=wgb[0][:], in_=wgb[0][:],
                                                func=AF.Identity, bias=m128c[:])
                            d_cvtg.inc(ins)
                        for c in range(NCH):
                            par = c % 2
                            sc.wait_ge(e_zld[c % 2].h, 16 * (c // 2 + 1))
                            if c >= 2:
                                sc.wait_ge(e_yg.h, 32 * (c - 1))
                            ins = sc.activation(out=zsl[par][:], in_=zch[par][:],
                                                func=AF.Silu)
                            e_zs.inc(ins)
                            for h in range(H):
                                gidx = 32 * c + h
                                sc.wait_ge(e_d1.h, gidx + 1)
                                if gidx >= 4:
                                    sc.wait_ge(e_mf.h, gidx - 3)
                                    sc.wait_ge(e_sn.h, gidx - 3)
                                ins = sc.activation(out=E_sb[gidx % 4][:], in_=D1[gidx % 4][:],
                                                    func=AF.Exp)
                                e_exp.inc(ins)
                                ins = sc.activation(out=Dinb[gidx % 4][:],
                                                    in_=ps_e[gidx % 3][0:64, :],
                                                    func=AF.Exp)
                                e_din.inc(ins)
                                act_dops(gidx)
                        if DO_D:
                            gg = (NG - 1) // 2
                            it, tc, _ = dgrp(2 * gg)
                            sc.wait_ge(d_mm.h, 2 * gg + 1)
                            ins = sc.activation(out=egb[gg % 2][:], in_=ps_g[:],
                                                func=AF.Exp, scale=ngsc[:, it:it + 1])
                            d_sg.inc(ins)

                    @blk.tensor
                    def _(t):
                        if "B" in phases:
                            t.wait_ge(b_bc.h, 17)
                        t.wait_ge(e_pp.h, 6)
                        for c in range(NCH):
                            t.wait_ge(e_scan.h, c + 1)
                            for k in range(4):
                                kk = 4 * c + k
                                if kk >= 3:
                                    t.wait_ge(e_ptev.h, kk - 2)
                                if k == 0:
                                    ins = t.transpose(ps_e[kk % 3][:, 0:32],
                                                      lamT_all[:, c * 128:(c + 1) * 128],
                                                      identf[0:32, 0:32])
                                elif k == 1:
                                    ins = t.transpose(ps_e[kk % 3][:, 0:32],
                                                      dtT[:, c * 128:(c + 1) * 128],
                                                      identf[0:32, 0:32])
                                elif k == 2:
                                    t.wait_ge(e_wt.h, c + 1)
                                    ins = t.transpose(ps_e[kk % 3][:, 0:32],
                                                      W_allTp[c % 2][:],
                                                      identf[0:32, 0:32])
                                else:
                                    ins = t.matmul(ps_e[kk % 3][:],
                                                   BT_bf[:, c * 128:(c + 1) * 128],
                                                   CT_bf[:, c * 128:(c + 1) * 128],
                                                   start=True, stop=True)
                                e_pt.inc(ins)
                            t.wait_ge(e_bev.h, c)
                            ins = t.transpose(ps_btm, BT_bf[:, c * 128:(c + 1) * 128],
                                              identb[0:64, 0:64])
                            e_btr.inc(ins)
                        t.wait_ge(e_ptev.h, 64)
                        t.wait_ge(e_lev.h, 1)
                        def pe_bcast(gidx):
                            ch = (gidx // 32) * 32 + (gidx % 32)
                            if gidx >= 3:
                                t.wait_ge(e_d1.h, gidx - 2)
                                t.wait_ge(e_din.h, gidx - 2)
                            ins = t.transpose(
                                ps_e[gidx % 3][:],
                                Lam_all[:, ch:ch + 1].broadcast_to([128, 128]),
                                identf[:])
                            e_bcx.inc(ins)
                        def pe_mm3(gidx):
                            c, h = gidx // 32, gidx % 32
                            t.wait_ge(e_bd.h, gidx + 1)
                            t.wait_ge(e_sn.h, gidx)
                            ins = t.matmul(ps_s[:], Bd[gidx % 4][:],
                                           xch[c % 2][:, 64 * h:64 * h + 64],
                                           start=True, stop=True)
                            e_smm.inc(ins)
                        def pe_dgrp(gidx):
                            if not DO_D:
                                return
                            it, tc, kind = dgrp(gidx)
                            gg = gidx // 2
                            if kind == 0:
                                t.wait_ge(d_cvtg.h, it + 1)
                                if gg >= 1:
                                    t.wait_ge(d_sgd.h, gg)
                                for k in range(NKT):
                                    ins = t.matmul(
                                        ps_g[:],
                                        wgb[it % 2][:, k * 128:(k + 1) * 128],
                                        xnT[:, k * T + tc * 512: k * T + tc * 512 + 512],
                                        start=(k == 0), stop=(k == NKT - 1))
                            else:
                                t.wait_ge(d_cvtu.h, it + 1)
                                if gg >= 1:
                                    t.wait_ge(d_h.h, gg)
                                for k in range(NKT):
                                    ins = t.matmul(
                                        ps_u[:],
                                        wub[it % 2][:, k * 128:(k + 1) * 128],
                                        xnT[:, k * T + tc * 512: k * T + tc * 512 + 512],
                                        start=(k == 0), stop=(k == NKT - 1))
                            d_mm.inc(ins)
                        pe_bcast(0); pe_bcast(1)
                        for c in range(NCH):
                            t.wait_ge(e_xld[c % 2].h, 16 * (c // 2 + 1))
                            for h in range(H):
                                hp = h % 2
                                gidx = 32 * c + h
                                if gidx + 2 < NG:
                                    pe_bcast(gidx + 2)
                                tp = (0, 64 * hp)
                                t.wait_ge(e_mf.h, gidx + 1)
                                if gidx >= 2:
                                    t.wait_ge(e_yg.h, gidx - 1)
                                rows = slice(64 * hp, 64 * hp + 64)
                                ins = t.matmul(ps_y[gidx % 2][rows, :],
                                               xch[c % 2][:, 64 * h:64 * h + 64],
                                               Mf[gidx % 4][:],
                                               start=True, stop=(c == 0),
                                               tile_position=tp)
                                if c > 0:
                                    t.wait_ge(e_cd.h, gidx + 1)
                                    t.wait_ge(e_sbf.h, 32 * (c - 1) + h + 1)
                                    ins = t.matmul(ps_y[gidx % 2][rows, :],
                                                   S_bf[:, 64 * h:64 * h + 64],
                                                   Cd[gidx % 4][:],
                                                   start=False, stop=True,
                                                   tile_position=tp)
                                e_ymm.inc(ins)
                                if gidx >= 1:
                                    pe_mm3(gidx - 1)
                                pe_dgrp(gidx)
                        pe_mm3(NG - 1)

                    @blk.sync
                    def _(s):
                        if "B" in phases:
                            s.wait_ge(b_zst[0].h, BZST0)
                            s.wait_ge(b_zst[1].h, BZST1)
                            s.wait_ge(b_xst[0].h, BXST0)
                            s.wait_ge(b_xst[1].h, BXST1)
                        s.wait_ge(e_pt.h, 64)
                        s.wait_ge(e_vt.h, NCH)
                        def loads(cc):
                            if cc >= NCH:
                                return
                            if cc >= 2:
                                s.wait_ge(e_zs.h, cc - 1)
                            s.dma_start(out=zch[cc % 2].ap().rearrange("p (j t) -> p j t", j=16),
                                        in_=z_sc[cc].rearrange("(j p) t -> p j t", p=128)
                                        ).then_inc(e_zld[cc % 2].h, 16)
                            e_zld[cc % 2].n += 16
                            if cc >= 2:
                                s.wait_ge(e_ymm.h, 32 * (cc - 1))
                                s.wait_ge(e_smm.h, 32 * (cc - 1))
                            s.dma_start(out=xch[cc % 2][:], in_=x_sc[cc]
                                        ).then_inc(e_xld[cc % 2].h, 16)
                            e_xld[cc % 2].n += 16
                        loads(0); loads(1)
                        for c in range(NCH):
                            if DO_D:
                                for gg in range(16 * c, 16 * c + 16):
                                    if 2 * gg + 2 <= NG:
                                        s.wait_ge(d_h.h, gg + 1)
                                        it, tc, _ = dgrp(2 * gg)
                                        for q in range(4):
                                            s.dma_start(
                                                out=h_sc[tc * 4 + q, it],
                                                in_=hst[gg % 2][:, q * 128:(q + 1) * 128]
                                            ).then_inc(d_hst[gg % 2].h, 16)
                                            d_hst[gg % 2].n += 16
                            loads(c + 2)

                    @blk.gpsimd
                    def _(g):
                        if DO_D:
                            if "B" in phases:
                                g.wait_ge(b_mm.h, b_mm.n)
                                g.wait_ge(b_sil.h, 17)
                            g.wait_ge(e_pt.h, 64)
                            g.wait_ge(e_wt.h, NCH)
                            g.wait_ge(e_scan.h, NCH)
                        def ygstore(c):
                            g.wait_ge(e_yg.h, 32 * (c + 1))
                            g.dma_start(out=yg_sc[c].rearrange("(j p) t -> p j t", p=128),
                                        in_=ygb[c % 2].ap().rearrange("p (j t) -> p j t", j=16)
                                        ).then_inc(e_ygst[c % 2].h, 16)
                            e_ygst[c % 2].n += 16
                        for c in range(NCH):
                            if DO_D:
                                for it in range(4 * c, 4 * c + 4):
                                    if it >= 2:
                                        g.wait_ge(d_mm.h, 8 * it - 8)
                                    g.dma_start(out=wgb[it % 2][:], in_=wg[it]
                                                ).then_inc(wgld[it % 2].h, 16)
                                    wgld[it % 2].n += 16
                                    g.dma_start(out=wub[it % 2][:], in_=wu[it]
                                                ).then_inc(wuld[it % 2].h, 16)
                                    wuld[it % 2].n += 16
                            if c >= 1:
                                ygstore(c - 1)
                        ygstore(NCH - 1)

        if "F" in phases:
            with ExitStack() as pstack:
                psum = lambda name, shape, dt: pstack.enter_context(nc.psum_tensor(name, shape, dt))
                ps_f = [psum(f"ps_f{i}", [128, 512], F32) for i in range(2)]
                EYG = e_yg.n; ESN = e_sn.n; EYMM = e_ymm.n; ESMM = e_smm.n
                EYGST0 = e_ygst[0].n; EYGST1 = e_ygst[1].n; ESBF = e_sbf.n
                with nc.Block() as blk:
                    @blk.gpsimd
                    def _(g):
                        if "E" in phases:
                            g.wait_ge(e_yg.h, EYG)
                            g.wait_ge(e_zs.h, NCH)
                            g.wait_ge(e_sn.h, ESN)
                            g.wait_ge(e_ymm.h, EYMM)
                            g.wait_ge(e_smm.h, ESMM)
                            g.wait_ge(e_ygst[0].h, EYGST0)
                            g.wait_ge(e_ygst[1].h, EYGST1)
                            g.wait_ge(e_sbf.h, ESBF)
                        for kt in range(NKT):
                            g.dma_start(out=WoutT[:, kt * T:(kt + 1) * T], in_=wout[kt]
                                        ).then_inc(f_wld.h, 16)
                            f_wld.n += 16

                    @blk.sync
                    def _(s):
                        if "E" in phases:
                            s.wait_ge(e_yg.h, e_yg.n)
                        for tt in range(NTT):
                            if "E" in phases:
                                s.wait_ge(e_ygst[tt % 2].h, 16 * (tt // 2 + 1))
                            if tt >= 2:
                                s.wait_ge(f_mm.h, 4 * (tt - 1))
                            s.dma_start(out=ygp[tt % 2].ap().rearrange("p (j t) -> p j t", j=16),
                                        in_=yg_sc[tt].rearrange("(j p) t -> p j t", p=128)
                                        ).then_inc(f_yld[tt % 2].h, 16)
                            f_yld[tt % 2].n += 16
                            for dc in range(4):
                                idx = 4 * tt + dc
                                s.wait_ge(f_sev.h, idx + 1)
                                s.dma_start(out=s_sc[tt][:, dc * 512:(dc + 1) * 512],
                                            in_=s_blk[idx % 2][:]
                                            ).then_inc(f_sst[idx % 2].h, 16)
                                f_sst[idx % 2].n += 16

                    @blk.tensor
                    def _(t):
                        t.wait_ge(f_wld.h, 16 * NKT)
                        for tt in range(NTT):
                            t.wait_ge(f_yld[tt % 2].h, 16 * (tt // 2 + 1))
                            for dc in range(4):
                                idx = 4 * tt + dc
                                if idx >= 2:
                                    t.wait_ge(f_sev.h, idx - 1)
                                for k in range(NKT):
                                    ins = t.matmul(
                                        ps_f[idx % 2][:],
                                        ygp[tt % 2][:, k * 128:(k + 1) * 128],
                                        WoutT[:, k * T + dc * 512: k * T + dc * 512 + 512],
                                        start=(k == 0), stop=(k == NKT - 1))
                                f_mm.inc(ins)

                    @blk.vector
                    def _(v):
                        for tt in range(NTT):
                            for dc in range(4):
                                idx = 4 * tt + dc
                                v.wait_ge(f_mm.h, idx + 1)
                                if idx >= 2:
                                    v.wait_ge(f_sst[idx % 2].h, 16 * (idx // 2))
                                ins = v.tensor_copy(out=s_blk[idx % 2][:],
                                                    in_=ps_f[idx % 2][:])
                                f_sev.inc(ins)

        if "G" in phases:
            with ExitStack() as pstack:
                psum = lambda name, shape, dt: pstack.enter_context(nc.psum_tensor(name, shape, dt))
                ps_dn = [psum(f"ps_dn{i}", [128, 512], F32) for i in range(8)]
                FMM = f_mm.n; FSST0 = f_sst[0].n; FSST1 = f_sst[1].n; DHST2a = d_hst[0].n; DHST2b = d_hst[1].n
                def gmidx(dc, ttg, half, ti):
                    return ((dc * 2 + ttg) * 2 + half) * 8 + ti + 1
                def silidx(dc, ttg, ti):
                    return (dc * 2 + ttg) * 8 + ti + 1
                with nc.Block() as blk:
                    @blk.gpsimd
                    def _(g):
                        if "F" in phases:
                            g.wait_ge(f_mm.h, FMM)
                        for ci in range(8):
                            dc, half = ci // 2, ci % 2
                            if ci >= 2:
                                g.wait_ge(g_mm.h, gmidx(dc - 1, 1, half, 7))
                            for j in range(32):
                                g.dma_start(out=wdp[ci % 2][:, j * 512:(j + 1) * 512],
                                            in_=wd[dc, half, j]).then_inc(g_wld[ci % 2].h, 16)
                                g_wld[ci % 2].n += 16


                    @blk.sync
                    def _(s):
                        if "D" in phases:
                            s.wait_ge(d_hst[0].h, DHST2a)
                            s.wait_ge(d_hst[1].h, DHST2b)
                        if "F" in phases:
                            s.wait_ge(f_sst[0].h, FSST0)
                            s.wait_ge(f_sst[1].h, FSST1)
                        for dc in range(4):
                            if dc >= 2:
                                s.wait_ge(g_scl.h, silidx(dc - 2, 1, 7))
                            s.dma_start(out=dscst[dc % 2][:], in_=dsc_d[dc]
                                        ).then_inc(g_dsc[dc % 2].h, 16)
                            g_dsc[dc % 2].n += 16
                            for ttg in range(2):
                                for half in range(2):
                                    for ti in range(8):
                                        tt = ttg * 8 + ti
                                        gi = gmidx(dc, ttg, half, ti)
                                        hb = (gi - 1) % 2
                                        if gi > 2:
                                            s.wait_ge(g_mm.h, gi - 2)
                                        s.dma_start(
                                            out=h_tt[hb].ap().rearrange("p (i t) -> p i t", i=32),
                                            in_=h_sc[tt, half * 32:(half + 1) * 32].rearrange(
                                                "i p t -> p i t")
                                        ).then_inc(g_hld[(gi - 1) % 2].h, 16)
                                        g_hld[(gi - 1) % 2].n += 16


                    @blk.tensor
                    def _(t):
                        for dc in range(4):
                            for ttg in range(2):
                                for half in range(2):
                                    ci = 2 * dc + half
                                    t.wait_ge(g_cvt.h, 32 * (ci + 1))
                                    for ti in range(8):
                                        gi = gmidx(dc, ttg, half, ti)
                                        hb = (gi - 1) % 2
                                        t.wait_ge(g_hld[(gi - 1) % 2].h, 16 * ((gi - 1) // 2 + 1))
                                        if half == 0:
                                            si_prev = silidx(dc, ttg, ti) - 8
                                            if si_prev > 0:
                                                t.wait_ge(g_scl.h, si_prev)
                                        for j in range(32):
                                            ins = t.matmul(
                                                ps_dn[ti][:],
                                                h_tt[hb][:, j * 128:(j + 1) * 128],
                                                wdp[ci % 2][:, j * 512:(j + 1) * 512],
                                                start=(half == 0 and j == 0),
                                                stop=(half == 1 and j == 31))
                                        g_mm.inc(ins)

                    @blk.scalar
                    def _(sc):
                        def cvt(ci):
                            dc2, half2 = ci // 2, ci % 2
                            if ci >= 2:
                                sc.wait_ge(g_mm.h, gmidx(dc2 - 1, 1, half2, 7))
                            sc.wait_ge(g_wld[ci % 2].h, 512 * (ci // 2 + 1))
                            for j in range(32):
                                ins = sc.activation(out=wdp[ci % 2][:, j * 512:(j + 1) * 512],
                                                    in_=wdp[ci % 2][:, j * 512:(j + 1) * 512],
                                                    func=AF.Identity, bias=m128c[:])
                                g_cvt.inc(ins)
                        cvt(0); cvt(1)
                        for dc in range(4):
                            for ttg in range(2):
                                for ti in range(8):
                                    tt = ttg * 8 + ti
                                    si = silidx(dc, ttg, ti)
                                    if si > 2:
                                        sc.wait_ge(g_ob.h, si - 2)
                                    sc.dma_start(out=sstrip[ti % 2][:],
                                                 in_=s_sc[tt][:, dc * 512:(dc + 1) * 512]
                                                 ).then_inc(g_sld[(si - 1) % 2].h, 16)
                                    g_sld[(si - 1) % 2].n += 16
                                    sc.dma_start(out=ustrip[ti % 2][:],
                                                 in_=u[tt * 128:(tt + 1) * 128,
                                                      dc * 512:(dc + 1) * 512]
                                                 ).then_inc(g_uld[(si - 1) % 2].h, 16)
                                    g_uld[(si - 1) % 2].n += 16
                                    sc.wait_ge(g_mm.h, gmidx(dc, ttg, 1, ti))
                                    sc.wait_ge(g_scl.h, si)
                                    ins = sc.activation(out=gsil[ti % 2][:],
                                                        in_=scld[ti % 2][:], func=AF.Silu)
                                    g_sil.inc(ins)
                                    sc.wait_ge(g_ob.h, si)
                                    sc.dma_start(out=y[tt * 128:(tt + 1) * 128,
                                                       dc * 512:(dc + 1) * 512],
                                                 in_=outb[ti % 2][:]
                                                 ).then_inc(g_ost[(si - 1) % 2].h, 16)
                                    g_ost[(si - 1) % 2].n += 16
                                if dc * 2 + ttg + 2 <= 7:
                                    cvt(dc * 2 + ttg + 2)
                        sc.wait_ge(g_ost[0].h, g_ost[0].n)
                        sc.wait_ge(g_ost[1].h, g_ost[1].n)

                    @blk.vector
                    def _(v):
                        for dc in range(4):
                            for ttg in range(2):
                                for ti in range(8):
                                    si = silidx(dc, ttg, ti)
                                    v.wait_ge(g_mm.h, gmidx(dc, ttg, 1, ti))
                                    v.wait_ge(g_dsc[dc % 2].h, 16 * (dc // 2 + 1))
                                    if si > 2:
                                        v.wait_ge(g_sil.h, si - 2)
                                    ins = v.tensor_tensor(out=scld[ti % 2][:],
                                                          in0=ps_dn[ti][:],
                                                          in1=dscst[dc % 2][:], op=ALU.mult)
                                    g_scl.inc(ins)
                                    v.wait_ge(g_sil.h, si)
                                    v.wait_ge(g_sld[(si - 1) % 2].h, 16 * ((si - 1) // 2 + 1))
                                    v.wait_ge(g_uld[(si - 1) % 2].h, 16 * ((si - 1) // 2 + 1))
                                    if si > 2:
                                        v.wait_ge(g_ost[(si - 1) % 2].h, 16 * ((si - 1) // 2))
                                    v.tensor_tensor(out=outb[ti % 2][:], in0=gsil[ti % 2][:],
                                                    in1=sstrip[ti % 2][:], op=ALU.mult)
                                    ins = v.tensor_tensor(out=outb[ti % 2][:],
                                                          in0=outb[ti % 2][:],
                                                          in1=ustrip[ti % 2][:], op=ALU.add)
                                    g_ob.inc(ins)

        if taps:
            tap_src = {
                "xnT": (lambda: xnT[:], lambda s: s.wait_ge(a_ev.h, a_ev.n)),
                "X_tm": (lambda: x_sc.rearrange("a p t -> (a p) t"),
                         lambda s: (s.wait_ge(b_xst[0].h, b_xst[0].n), s.wait_ge(b_xst[1].h, b_xst[1].n))),
                "BT_bf": (lambda: BT_bf[:], lambda s: s.wait_ge(b_bc.h, b_bc.n)),
                "CT_bf": (lambda: CT_bf[:], lambda s: s.wait_ge(b_bc.h, b_bc.n)),
                "dtraw": (lambda: dtraw[:], lambda s: s.wait_ge(b_ev.h, b_ev.n)),
                "z_sc": (lambda: z_sc.rearrange("c d t -> (c d) t"),
                         lambda s: (s.wait_ge(b_zst[0].h, b_zst[0].n), s.wait_ge(b_zst[1].h, b_zst[1].n))),
                "yg_sc": (lambda: yg_sc.rearrange("c d t -> (c d) t"),
                          lambda s: (s.wait_ge(e_ygst[0].h, e_ygst[0].n), s.wait_ge(e_ygst[1].h, e_ygst[1].n))),
                "h_sc": (lambda: h_sc.rearrange("a b p t -> (a b p) t"),
                         lambda s: (s.wait_ge(d_hst[0].h, d_hst[0].n), s.wait_ge(d_hst[1].h, d_hst[1].n))),
                "s_sc": (lambda: s_sc.rearrange("a p t -> (a p) t"),
                         lambda s: (s.wait_ge(f_sst[0].h, f_sst[0].n), s.wait_ge(f_sst[1].h, f_sst[1].n))),
            }
            with nc.Block() as blk:
                @blk.sync
                def _(s):
                    done = sem("tapdone")
                    for tname, shape, dt in taps:
                        src, wait = tap_src[tname]
                        if wait is not None:
                            wait(s)
                        s.dma_start(out=taps_out[tname], in_=src()).then_inc(done.h, 16)
                        done.n += 16
                    s.wait_ge(done.h, done.n)

    return nc



def _prep_common(L0_attn_W_in, L0_attn_conv_w, L0_attn_conv_b, L0_attn_W_out,
                 L0_attn_A_log, L0_attn_D_param, L0_attn_dt_bias,
                 L0_ffn_gate_w, L0_ffn_gate_scale, L0_ffn_up_w, L0_ffn_up_scale,
                 L0_ffn_down_w, L0_ffn_down_scale):
    f = np.float32
    win_p = np.zeros((D, NJT * 128), f)
    win_p[:, :JPROJ] = np.asarray(L0_attn_W_in, f).T
    win_t = np.ascontiguousarray(
        win_p.reshape(NKT, 128, NJT, 128).transpose(2, 1, 0, 3).reshape(NJT, 128, NKT * 128))
    woutT = np.asarray(L0_attn_W_out, f).T
    wout_t = np.ascontiguousarray(woutT.reshape(NKT, 128, D))
    wgT = np.asarray(L0_ffn_gate_w, np.int32).T
    wg_t = np.ascontiguousarray(
        wgT.reshape(NKT, 128, NIT, 128).transpose(2, 1, 0, 3).reshape(NIT, 128, NKT * 128))
    wuT = np.asarray(L0_ffn_up_w, np.int32).T
    wu_t = np.ascontiguousarray(
        wuT.reshape(NKT, 128, NIT, 128).transpose(2, 1, 0, 3).reshape(NIT, 128, NKT * 128))
    wdT = np.asarray(L0_ffn_down_w, np.int32).T
    wd_t = np.ascontiguousarray(
        wdT.reshape(2, 32, 128, 4, 512).transpose(3, 0, 1, 2, 4))
    gsc = np.ascontiguousarray(np.asarray(L0_ffn_gate_scale, f).reshape(NIT, 128).T)
    usc = np.ascontiguousarray(np.asarray(L0_ffn_up_scale, f).reshape(NIT, 128).T)
    dsc = np.ascontiguousarray(
        np.broadcast_to(np.asarray(L0_ffn_down_scale, f).reshape(4, 1, 512), (4, 128, 512)))
    cw = np.ascontiguousarray(np.asarray(L0_attn_conv_w, f)[:, 0, :].reshape(17, 128, 4))
    cb = np.ascontiguousarray(np.asarray(L0_attn_conv_b, f).reshape(17, 128, 1))
    dtbv = np.ascontiguousarray(np.asarray(L0_attn_dt_bias, f).reshape(32, 1))
    alogv = np.ascontiguousarray(np.asarray(L0_attn_A_log, f).reshape(32, 1))
    ddiag = np.ascontiguousarray(np.tile(np.asarray(L0_attn_D_param, f), (128, 1)))
    triv = np.triu(np.ones((128, 128), f))
    idf = np.eye(128, dtype=f)
    eps = np.full((128, 1), 1e-5, f)
    ones = np.ones((128, 128), f)
    return dict(win=win_t, wout=wout_t, wg=wg_t, wu=wu_t, wd=wd_t,
                gsc=gsc, usc=usc, dsc=dsc, cw=cw, cb=cb, dtb=dtbv, alog=alogv,
                ddiag=ddiag, tri=triv, idf=idf, eps=eps, onesd=ones,
                m128d=np.full((128, 1), -128.0, f))


def kernel(**inputs):
    common = _prep_common(**{k: v for k, v in inputs.items() if k != "L0_attn_u"})
    u_all = np.ascontiguousarray(np.asarray(inputs["L0_attn_u"], np.float32))
    nc = build()
    in_maps = [dict(common, u=u_all[b]) for b in range(8)]
    res = run_bass_kernel_spmd(nc, in_maps, core_ids=list(range(8)))
    return np.stack([r["y"] for r in res.results]).astype(np.float32)


# version 18
# speedup vs baseline: 1.2315x; 1.2315x over previous
import numpy as np
import ml_dtypes
from contextlib import ExitStack
import concourse.bass as bass
import concourse.mybir as mybir
from concourse.bass_utils import run_bass_kernel_spmd

F32 = mybir.dt.float32
BF16 = mybir.dt.bfloat16
I32 = mybir.dt.int32
U8 = mybir.dt.uint8
AF = mybir.ActivationFunctionType
ALU = mybir.AluOpType

T = 2048
D = 2048
NJT = 34
JPROJ = 4256
H = 32
Q = 128
NCH = 16
NTT = 16
NKT = 16
NIT = 64
NTC = 4


class Sem:
    def __init__(self, stack, nc, name):
        self.h = stack.enter_context(nc.semaphore(name))
        self.n = 0

    def inc(self, instr, k=1):
        instr.then_inc(self.h, k)
        self.n += k
        return self.n


def _carve(nc, arena, name, shape, dtype, off):
    assert off % 64 == 0
    hnd = nc.virtual_sbuf_tensor(name, shape, dtype)
    nc._tensor(hnd.name, hnd.shape, hnd.dtype, type="SB")
    mloc = nc.lookup_mloc(hnd)
    mloc.allocated = not nc.target_bir_lowering
    mloc.addr = nc.lookup_mloc(arena).addr + off
    nbytes = int(np.prod(shape[1:])) * mybir.dt.size(dtype)
    assert off + nbytes <= arena.shape[1], (name, off, nbytes, arena.shape)
    return hnd


def build(phases=("A", "B", "D", "E", "F", "G"), taps=()):
    nc = bass.Bass("TRN2", target_bir_lowering=False, debug=False)

    u = nc.dram_tensor("u", [T, D], F32, kind="ExternalInput").ap()
    win = nc.dram_tensor("win", [NJT, 128, NKT * 128], F32, kind="ExternalInput").ap()
    wout = nc.dram_tensor("wout", [NKT, 128, D], F32, kind="ExternalInput").ap()
    wg = nc.dram_tensor("wg", [NIT, 128, NKT * 128], I32, kind="ExternalInput").ap()
    wu = nc.dram_tensor("wu", [NIT, 128, NKT * 128], I32, kind="ExternalInput").ap()
    wd = nc.dram_tensor("wd", [4, 2, 32, 128, 512], I32, kind="ExternalInput").ap()
    gsc_d = nc.dram_tensor("gsc", [128, NIT], F32, kind="ExternalInput").ap()
    usc_d = nc.dram_tensor("usc", [128, NIT], F32, kind="ExternalInput").ap()
    dsc_d = nc.dram_tensor("dsc", [4, 128, 512], F32, kind="ExternalInput").ap()
    cw_d = nc.dram_tensor("cw", [17, 128, 4], F32, kind="ExternalInput").ap()
    cb_d = nc.dram_tensor("cb", [17, 128, 1], F32, kind="ExternalInput").ap()
    dtb_d = nc.dram_tensor("dtb", [32, 1], F32, kind="ExternalInput").ap()
    alog_d = nc.dram_tensor("alog", [32, 1], F32, kind="ExternalInput").ap()
    ddiag_d = nc.dram_tensor("ddiag", [128, 32], F32, kind="ExternalInput").ap()
    tri_d = nc.dram_tensor("tri", [128, 128], F32, kind="ExternalInput").ap()
    idf_d = nc.dram_tensor("idf", [128, 128], F32, kind="ExternalInput").ap()
    eps_d = nc.dram_tensor("eps", [128, 1], F32, kind="ExternalInput").ap()
    ones_d = nc.dram_tensor("onesd", [128, 128], F32, kind="ExternalInput").ap()
    m128_d = nc.dram_tensor("m128d", [128, 1], F32, kind="ExternalInput").ap()
    mneg_d = nc.dram_tensor("mnegd", [128, 128], F32, kind="ExternalInput").ap()

    y = nc.dram_tensor("y", [T, D], F32, kind="ExternalOutput").ap()

    z_sc = nc.dram_tensor("z_sc", [NCH, D, 128], BF16, kind="Internal").ap()
    h_sc = nc.dram_tensor("h_sc", [NTT, NIT, 128, 128], BF16, kind="Internal").ap()
    yg_sc = nc.dram_tensor("yg_sc", [NCH, D, 128], BF16, kind="Internal").ap()
    s_sc = nc.dram_tensor("s_sc", [NTT, 128, D], BF16, kind="Internal").ap()

    taps_out = {}
    for tname, shape, dt in taps:
        taps_out[tname] = nc.dram_tensor("tap_" + tname, shape, dt,
                                         kind="ExternalOutput").ap()

    stack = ExitStack()
    with stack:
        sbuf = lambda name, shape, dt: stack.enter_context(nc.sbuf_tensor(name, shape, dt))
        sem = lambda name: Sem(stack, nc, name)

        ar1 = sbuf("arena1", [128, 65536], U8)
        ar2 = sbuf("arena2", [128, 65536], U8)
        ar3 = sbuf("arena3", [128, 28672], U8)
        cv = lambda a, n, s, d, o: _carve(nc, a, n, s, d, o)

        BT_bf = sbuf("BT_bf", [64, T], BF16)
        CT_bf = sbuf("CT_bf", [64, T], BF16)
        dtraw = sbuf("dtraw", [32, T], F32)
        identb = sbuf("identb", [128, 128], BF16)
        mnegT = sbuf("mnegT", [128, 128], BF16)
        identf = sbuf("identf", [128, 128], F32)
        tri = sbuf("tri_s", [128, 128], F32)
        onesf = sbuf("onesf", [128, 128], F32)
        gsc = sbuf("gsc_s", [128, NIT], F32)
        usc = sbuf("usc_s", [128, NIT], F32)
        cwt = sbuf("cwt", [128, 17 * 4], F32)
        cbt = sbuf("cbt", [128, 17], F32)
        dtb = sbuf("dtb_s", [32, 1], F32)
        alog = sbuf("alog_s", [32, 1], F32)
        ddiag = sbuf("ddiag_s", [128, 32], F32)
        epsc = sbuf("epsc", [128, 1], F32)
        m128c = sbuf("m128c", [128, 1], F32)
        stats = sbuf("stats", [128, 24], F32)
        aggr = sbuf("aggr", [128, 2], F32)
        sqv = sbuf("sqv", [128, 1], F32)
        rstd = sbuf("rstd", [128, 1], F32)
        nmr = sbuf("nmr", [128, 1], F32)
        a_tmp = sbuf("a_tmp", [32, 1], F32)
        A_col = sbuf("A_col", [32, 1], F32)

        xnT = cv(ar1, "xnT", [128, NKT * T], BF16, 0)
        X_tm = cv(ar2, "X_tm", [128, NTT * D], BF16, 0)
        u_t = [cv(ar3, f"u_t{i}", [128, T], F32, i * 8192) for i in range(2)]
        xn_t = [cv(ar3, f"xn_t{i}", [128, T], BF16, 16384 + i * 4096) for i in range(2)]
        winb = [cv(ar3, f"winb{i}", [128, NKT * 128], BF16, i * 4096) for i in range(2)]
        padb = [cv(ar3, f"pad{i}", [128, T + 3], BF16, 8192 + i * 4352) for i in range(2)]
        acc = cv(ar3, "acc", [128, T], F32, 16896)
        zst = [cv(ar3, f"zst{i}", [128, 512], BF16, 25088 + i * 1024) for i in range(2)]
        wgb = [cv(ar3, f"wgb{i}", [128, NKT * 128], BF16, i * 4096) for i in range(2)]
        wub = [cv(ar3, f"wub{i}", [128, NKT * 128], BF16, 8192 + i * 4096) for i in range(2)]
        sgb = [cv(ar3, f"sgb{i}", [128, 512], F32, 16384 + i * 2048) for i in range(2)]
        hst = [cv(ar3, f"hst{i}", [128, 512], BF16, 20480 + i * 1024) for i in range(2)]
        dtT = cv(ar3, "dtT", [32, T], F32, 0)
        alphaT = cv(ar3, "alphaT", [32, T], F32, 8192)
        spx = cv(ar3, "spx", [32, T], F32, 16384)
        lamT_all = cv(ar3, "lamT_all", [32, T], F32, 16384)
        zsl = [cv(ar3, f"zsl{i}", [128, NKT * 128], F32, i * 8192) for i in range(2)]
        ygb = [cv(ar1, f"ygb{i}", [128, NKT * 128], BF16, i * 4096) for i in range(2)]
        S_f = cv(ar1, "S_f", [64, H * 64], F32, 8192)
        S_bf = cv(ar1, "S_bf", [64, H * 64], BF16, 16384)
        zch = [cv(ar1, f"zch{i}", [128, NKT * 128], BF16, 20480 + i * 4096) for i in range(2)]
        Lam_all = cv(ar1, "Lam_all", [128, NCH * 32], F32, 28672)
        negLam_all = cv(ar1, "negLam_all", [128, NCH * 32], F32, 30720)
        dt_tm_all = cv(ar1, "dt_tm_all", [128, NCH * 32], F32, 32768)
        W_all_all = cv(ar1, "W_all_all", [128, NCH * 32], F32, 34816)
        B_tm_all = cv(ar1, "B_tm_all", [128, NCH * 64], BF16, 36864)
        GT_all = cv(ar1, "GT_all", [128, NCH * 128], F32, 38912)
        eoff = [47104]
        def ecv(name, shape, dt, n=1):
            outs = []
            for i in range(n):
                nb = int(np.prod(shape[1:])) * mybir.dt.size(dt)
                nb = (nb + 63) // 64 * 64
                outs.append(cv(ar1, f"{name}{i}", shape, dt, eoff[0]))
                eoff[0] += nb
            return outs
        V_Tp = ecv("V_Tp", [32, 128], F32, 2)
        W_allTp = ecv("W_allTp", [32, 128], F32, 2)
        D1 = ecv("D1", [128, 128], F32, 4)
        E_sb = ecv("E_sb", [128, 128], F32, 4)
        t1 = ecv("t1", [128, 128], F32, 2)
        Mf = ecv("Mf", [128, 128], BF16, 4)
        Cd = ecv("Cd", [64, 128], BF16, 4)
        Bd = ecv("Bd", [128, 64], BF16, 4)
        Dinb = ecv("Dinb", [64, 128], F32, 4)
        assert eoff[0] <= 65536
        WoutT = cv(ar1, "WoutT", [128, NKT * T], BF16, 0)
        ygp = [cv(ar3, f"ygp{i}", [128, NKT * 128], BF16, i * 4096) for i in range(2)]
        s_blk = [cv(ar3, f"s_blk{i}", [128, 512], BF16, 8192 + i * 1024) for i in range(2)]
        wdp = [cv(ar1, f"wdp{i}", [128, 32 * 512], BF16, i * 32768) for i in range(2)]
        h_tt = [cv(ar2, f"h_tt{i}", [128, 32 * 128], BF16, i * 8192) for i in range(2)]
        dscst = [cv(ar2, f"dscst{i}", [128, 512], F32, 16384 + i * 2048) for i in range(2)]
        sstrip = [cv(ar2, f"sstrip{i}", [128, 512], BF16, 20480 + i * 1024) for i in range(2)]
        ustrip = [cv(ar2, f"ustrip{i}", [128, 512], F32, 22528 + i * 2048) for i in range(2)]
        gsil = [cv(ar2, f"gsil{i}", [128, 512], F32, 26624 + i * 2048) for i in range(2)]
        outb = [cv(ar2, f"outb{i}", [128, 512], F32, 30720 + i * 2048) for i in range(2)]
        scld = [cv(ar2, f"scld{i}", [128, 512], F32, 34816 + i * 2048) for i in range(2)]

        c_ld = sem("c_ld"); c_ldg = sem("c_ldg")

        with nc.Block() as blk:
            @blk.gpsimd
            def _(g):
                g.dma_start(out=identb[:], in_=idf_d).then_inc(c_ldg.h, 16)
                c_ldg.n += 16
                g.dma_start(out=mnegT[:], in_=mneg_d).then_inc(c_ldg.h, 16)
                c_ldg.n += 16
            @blk.sync
            def _(s):
                for dst, src in ((identf, idf_d), (tri, tri_d), (onesf, ones_d),
                                 (gsc, gsc_d), (usc, usc_d),
                                 (dtb, dtb_d), (alog, alog_d), (ddiag, ddiag_d),
                                 (epsc, eps_d), (m128c, m128_d)):
                    s.dma_start(out=dst[:], in_=src).then_inc(c_ld.h, 16)
                    c_ld.n += 16
                s.dma_start(out=cwt.ap().rearrange("p (c k) -> p c k", c=17),
                            in_=cw_d.rearrange("c p k -> p c k")).then_inc(c_ld.h, 16)
                c_ld.n += 16
                s.dma_start(out=cbt.ap().rearrange("p (c o) -> p c o", c=17),
                            in_=cb_d.rearrange("c p o -> p c o"),
                            allow_slow_non_contiguous=True).then_inc(c_ld.h, 16)
                c_ld.n += 16
        CONST = c_ld.n; CONSTG = c_ldg.n

        a_ld = [sem("a_ld0"), sem("a_ld1")]; a_stats = sem("a_st"); a_ag = sem("a_ag")
        a_xn = sem("a_xn"); a_tr = sem("a_tr"); a_ev = sem("a_ev")
        b_wld = [sem("b_wld0"), sem("b_wld1")]; b_mm = sem("b_mm"); b_ev = sem("b_ev")
        b_zst = [sem("b_zst0"), sem("b_zst1")]; b_cv = sem("b_cv"); b_sil = sem("b_sil")
        b_xtr = sem("b_xtr"); b_xev = sem("b_xev"); b_bc = sem("b_bc")
        b_bcv = sem("b_bcv")
        wgld = [sem("wgld0"), sem("wgld1")]; wuld = [sem("wuld0"), sem("wuld1")]
        d_cvtg = sem("d_cvtg"); d_cvtu = sem("d_cvtu"); d_mm = sem("d_mm")
        d_sg = sem("d_sg"); d_h = sem("d_h"); d_hst = [sem("d_hst0"), sem("d_hst1")]
        e_pp = sem("e_pp"); e_scan = sem("e_scan"); e_pt = sem("e_pt")
        e_ptev = sem("e_ptev"); e_lev = sem("e_lev"); e_vt = sem("e_vt")
        e_wt = sem("e_wt"); e_btr = sem("e_btr"); e_bev = sem("e_bev")
        e_gmm = sem("e_gmm"); e_gt = sem("e_gt"); e_zld = [sem("e_zld0"), sem("e_zld1")]
        e_ygst = [sem("e_ygst0"), sem("e_ygst1")]; e_bcx = sem("e_bcx"); e_d1 = sem("e_d1")
        e_exp = sem("e_exp"); e_din = sem("e_din"); e_mf = sem("e_mf")
        e_cd = sem("e_cd"); e_bd = sem("e_bd"); e_ymm = sem("e_ymm")
        e_smm = sem("e_smm"); e_sn = sem("e_sn"); e_sbf = sem("e_sbf")
        e_zs = sem("e_zs"); e_yg = sem("e_yg")
        f_wld = sem("f_wld"); f_yld = [sem("f_yld0"), sem("f_yld1")]; f_mm = sem("f_mm")
        f_sev = sem("f_sev"); f_sst = [sem("f_sst0"), sem("f_sst1")]
        g_dsc = [sem("g_dsc0"), sem("g_dsc1")]; g_wld = [sem("g_wld0"), sem("g_wld1")]
        g_cvt = sem("g_cvt")
        g_hld = [sem("g_hld0"), sem("g_hld1")]; g_mm = sem("g_mm"); g_sil = sem("g_sil")
        g_sld = [sem("g_sld0"), sem("g_sld1")]; g_uld = [sem("g_uld0"), sem("g_uld1")]
        g_ob = sem("g_ob")
        g_scl = sem("g_scl")
        g_ost = [sem("g_ost0"), sem("g_ost1")]

        if "A" in phases:
            with ExitStack() as pstack:
                psum = lambda name, shape, dt: pstack.enter_context(nc.psum_tensor(name, shape, dt))
                ps_tr = [psum(f"ps_tr{i}", [128, 128], BF16) for i in range(2)]
                with nc.Block() as blk:
                    @blk.sync
                    def _(s):
                        for tt in range(NTT):
                            if tt >= 2:
                                s.wait_ge(a_xn.h, tt - 1)
                            s.dma_start(out=u_t[tt % 2][:],
                                        in_=u[tt * 128:(tt + 1) * 128, :]
                                        ).then_inc(a_ld[tt % 2].h, 16)
                            a_ld[tt % 2].n += 16

                    @blk.vector
                    def _(v):
                        v.wait_ge(c_ld.h, CONST)
                        def evacs(tt):
                            for kt in range(NKT):
                                g = tt * NKT + kt
                                v.wait_ge(a_tr.h, g + 1)
                                ins = v.tensor_copy(
                                    out=xnT[:, kt * T + tt * 128: kt * T + (tt + 1) * 128],
                                    in_=ps_tr[g % 2][:])
                                a_ev.inc(ins)
                        for tt in range(NTT):
                            v.wait_ge(a_ld[tt % 2].h, 16 * (tt // 2 + 1))
                            for qq in range(4):
                                ins = v.bn_stats(out=stats[:, 6 * qq:6 * qq + 6],
                                                 in_=u_t[tt % 2][:, 512 * qq:512 * (qq + 1)])
                            a_stats.inc(ins)
                            v.wait_ge(a_stats.h, tt + 1)
                            ins = v.bn_aggr(out=aggr[:], in_=stats[:])
                            a_ag.inc(ins)
                            v.wait_ge(a_ag.h, 4 * tt + 2)
                            ins = v.reciprocal(out=rstd[:], in_=sqv[:])
                            a_ag.inc(ins)
                            v.wait_ge(a_ag.h, 4 * tt + 3)
                            ins = v.tensor_scalar(out=nmr[:], in0=aggr[:, 0:1],
                                                  scalar1=rstd[:], scalar2=-1.0,
                                                  op0=ALU.mult, op1=ALU.mult)
                            a_ag.inc(ins)
                            if tt >= 1:
                                evacs(tt - 1)
                        evacs(NTT - 1)

                    @blk.scalar
                    def _(sc):
                        sc.wait_ge(c_ld.h, CONST)
                        for tt in range(NTT):
                            sc.wait_ge(a_ag.h, 4 * tt + 1)
                            ins = sc.activation(out=sqv[:], in_=aggr[:, 1:2],
                                                func=AF.Sqrt, bias=epsc[:])
                            a_ag.inc(ins)
                            sc.wait_ge(a_ag.h, 4 * tt + 4)
                            if tt >= 2:
                                sc.wait_ge(a_tr.h, 16 * (tt - 1))
                            ins = sc.activation(out=xn_t[tt % 2][:], in_=u_t[tt % 2][:],
                                                func=AF.Identity, bias=nmr[:], scale=rstd[:])
                            a_xn.inc(ins)

                    @blk.tensor
                    def _(t):
                        t.wait_ge(c_ld.h, CONST)
                        t.wait_ge(c_ldg.h, CONSTG)
                        for tt in range(NTT):
                            t.wait_ge(a_xn.h, tt + 1)
                            for kt in range(NKT):
                                g = tt * NKT + kt
                                if g >= 2:
                                    t.wait_ge(a_ev.h, g - 1)
                                ins = t.transpose(ps_tr[g % 2][:],
                                                  xn_t[tt % 2][:, kt * 128:(kt + 1) * 128],
                                                  identb[:])
                                a_tr.inc(ins)

        XT = list(range(16, 32))
        if "B" in phases:
            with ExitStack() as pstack:
                psum = lambda name, shape, dt: pstack.enter_context(nc.psum_tensor(name, shape, dt))
                ps_pj = [psum(f"ps_pj{i}", [128, 512], F32) for i in range(2)]
                ps_xtr = [psum(f"ps_xtr{i}", [128, 128], F32) for i in range(2)]
                with nc.Block() as blk:
                    @blk.gpsimd
                    def _(g):
                        if "A" in phases:
                            g.wait_ge(a_xn.h, NTT)
                            g.wait_ge(a_tr.h, 256)
                        for jt in range(NJT):
                            if jt >= 2:
                                g.wait_ge(b_mm.h, 4 * jt - 4)
                            g.dma_start(out=winb[jt % 2][:], in_=win[jt]
                                        ).then_inc(b_wld[jt % 2].h, 16)
                            b_wld[jt % 2].n += 16
                        g.wait_ge(b_sil.h, 17)
                        g.dma_start(out=CT_bf[:], in_=acc[64:128, :]).then_inc(b_bc.h, 16)
                        b_bc.n += 16

                    @blk.tensor
                    def _(t):
                        t.wait_ge(a_ev.h, a_ev.n)
                        def emit_xtr(jtx):
                            t.wait_ge(b_sil.h, jtx - 15)
                            for tt in range(NTT):
                                gx = (jtx - 16) * NTT + tt
                                if gx >= 2:
                                    t.wait_ge(b_xev.h, gx - 1)
                                ins = t.transpose(ps_xtr[gx % 2][:],
                                                  acc[:, tt * 128:(tt + 1) * 128],
                                                  identf[:])
                                b_xtr.inc(ins)
                        for jt in range(NJT):
                            t.wait_ge(b_wld[jt % 2].h, 16 * (jt // 2 + 1))
                            for tc in range(NTC):
                                idx = 4 * jt + tc
                                if idx >= 2:
                                    t.wait_ge(b_ev.h, idx - 1)
                                for k in range(NKT):
                                    ins = t.matmul(
                                        ps_pj[idx % 2][:],
                                        winb[jt % 2][:, k * 128:(k + 1) * 128],
                                        xnT[:, k * T + tc * 512: k * T + tc * 512 + 512],
                                        start=(k == 0), stop=(k == NKT - 1))
                                b_mm.inc(ins)
                            if jt - 1 in XT:
                                emit_xtr(jt - 1)

                    @blk.vector
                    def _(v):
                        v.wait_ge(c_ld.h, CONST)
                        v.memset(padb[0][:, 0:3], 0.0)
                        v.memset(padb[1][:, 0:3], 0.0)
                        for jt in range(NJT):
                            for tc in range(NTC):
                                idx = 4 * jt + tc
                                v.wait_ge(b_mm.h, idx + 1)
                                if jt <= 15:
                                    if idx >= 2:
                                        v.wait_ge(b_zst[idx % 2].h, 64 * (idx // 2))
                                    ins = v.tensor_copy(out=zst[idx % 2][:],
                                                        in_=ps_pj[idx % 2][:])
                                elif jt <= 32:
                                    ins = v.tensor_copy(
                                        out=padb[jt % 2][:, 3 + tc * 512: 3 + tc * 512 + 512],
                                        in_=ps_pj[idx % 2][:])
                                else:
                                    ins = v.tensor_copy(
                                        out=dtraw[0:32, tc * 512:(tc + 1) * 512],
                                        in_=ps_pj[idx % 2][0:32, :])
                                b_ev.inc(ins)
                            if jt - 1 in XT:
                                for tt in range(NTT):
                                    gx = (jt - 17) * NTT + tt
                                    v.wait_ge(b_xtr.h, gx + 1)
                                    ins = v.tensor_copy(
                                        out=X_tm[:, tt * D + (jt - 17) * 128:
                                                 tt * D + (jt - 17) * 128 + 128],
                                        in_=ps_xtr[gx % 2][:])
                                    b_xev.inc(ins)
                            if 16 <= jt <= 32:
                                ct = jt - 16
                                v.wait_ge(b_ev.h, 4 * jt + 4)
                                if jt >= 17 and (jt - 1) in XT:
                                    v.wait_ge(b_xtr.h, 16 * (jt - 16))
                                pb = padb[jt % 2]
                                v.tensor_scalar(out=acc[:], in0=pb[:, 0:T],
                                                scalar1=cwt[:, ct * 4: ct * 4 + 1],
                                                scalar2=None, op0=ALU.mult)
                                for k in range(1, 4):
                                    ins = v.scalar_tensor_tensor(
                                        out=acc[:], in0=pb[:, k:k + T],
                                        scalar=cwt[:, ct * 4 + k: ct * 4 + k + 1],
                                        in1=acc[:], op0=ALU.mult, op1=ALU.add)
                                b_cv.inc(ins)
                        v.wait_ge(b_sil.h, 17)
                        ins = v.tensor_copy(out=BT_bf[:], in_=acc[0:64, :])
                        b_bcv.inc(ins)

                    @blk.scalar
                    def _(sc):
                        sc.wait_ge(c_ld.h, CONST)
                        for jtx in range(16, 33):
                            ct = jtx - 16
                            sc.wait_ge(b_cv.h, ct + 1)
                            ins = sc.activation(out=acc[:], in_=acc[:], func=AF.Silu,
                                                bias=cbt[:, ct:ct + 1])
                            b_sil.inc(ins)

                    @blk.sync
                    def _(s):
                        for jt in range(16):
                            for tc in range(NTC):
                                idx = 4 * jt + tc
                                s.wait_ge(b_ev.h, idx + 1)
                                for q in range(4):
                                    s.dma_start(
                                        out=z_sc[tc * 4 + q, jt * 128:(jt + 1) * 128, :],
                                        in_=zst[idx % 2][:, q * 128:(q + 1) * 128]
                                    ).then_inc(b_zst[idx % 2].h, 16)
                                    b_zst[idx % 2].n += 16

        if "D" in phases:
            with ExitStack() as pstack:
                psum = lambda name, shape, dt: pstack.enter_context(nc.psum_tensor(name, shape, dt))
                ps_g = [psum(f"ps_g{i}", [128, 512], F32) for i in range(2)]
                ps_u = [psum(f"ps_u{i}", [128, 512], F32) for i in range(2)]
                with nc.Block() as blk:
                    @blk.gpsimd
                    def _(g):
                        if "B" in phases:
                            g.wait_ge(b_mm.h, b_mm.n)
                            g.wait_ge(b_sil.h, 17)
                        for it in range(NIT):
                            if it >= 2:
                                g.wait_ge(d_mm.h, 8 * it - 8)
                            g.dma_start(out=wgb[it % 2][:], in_=wg[it]
                                        ).then_inc(wgld[it % 2].h, 16)
                            wgld[it % 2].n += 16
                            g.dma_start(out=wub[it % 2][:], in_=wu[it]
                                        ).then_inc(wuld[it % 2].h, 16)
                            wuld[it % 2].n += 16


                    @blk.tensor
                    def _(t):
                        t.wait_ge(a_ev.h, a_ev.n)
                        for it in range(NIT):
                            t.wait_ge(d_cvtg.h, it + 1)
                            t.wait_ge(d_cvtu.h, it + 1)
                            for tc in range(NTC):
                                idxp = it * 4 + tc
                                if idxp >= 2:
                                    t.wait_ge(d_sg.h, idxp - 1)
                                    t.wait_ge(d_h.h, idxp - 1)
                                for k in range(NKT):
                                    ins = t.matmul(
                                        ps_g[idxp % 2][:],
                                        wgb[it % 2][:, k * 128:(k + 1) * 128],
                                        xnT[:, k * T + tc * 512: k * T + tc * 512 + 512],
                                        start=(k == 0), stop=(k == NKT - 1))
                                d_mm.inc(ins)
                                for k in range(NKT):
                                    ins = t.matmul(
                                        ps_u[idxp % 2][:],
                                        wub[it % 2][:, k * 128:(k + 1) * 128],
                                        xnT[:, k * T + tc * 512: k * T + tc * 512 + 512],
                                        start=(k == 0), stop=(k == NKT - 1))
                                d_mm.inc(ins)

                    @blk.scalar
                    def _(sc):
                        sc.wait_ge(c_ld.h, CONST)
                        if "B" in phases:
                            sc.wait_ge(b_xtr.h, 256)
                            sc.wait_ge(b_bc.h, b_bc.n)
                            sc.wait_ge(b_bcv.h, b_bcv.n)
                        for it in range(NIT):
                            sc.wait_ge(wgld[it % 2].h, 16 * (it // 2 + 1))
                            ins = sc.activation(out=wgb[it % 2][:], in_=wgb[it % 2][:],
                                                func=AF.Identity, bias=m128c[:])
                            d_cvtg.inc(ins)
                            for tc in range(NTC):
                                idxp = it * 4 + tc
                                sc.wait_ge(d_mm.h, 2 * idxp + 1)
                                if idxp >= 2:
                                    sc.wait_ge(d_h.h, idxp - 1)
                                ins = sc.activation(out=sgb[idxp % 2][:],
                                                    in_=ps_g[idxp % 2][:],
                                                    func=AF.Silu,
                                                    scale=gsc[:, it:it + 1])
                                d_sg.inc(ins)

                    @blk.vector
                    def _(v):
                        v.wait_ge(c_ld.h, CONST)
                        if "B" in phases:
                            v.wait_ge(b_bc.h, b_bc.n)
                            v.wait_ge(b_bcv.h, b_bcv.n)
                        for it in range(NIT):
                            v.wait_ge(wuld[it % 2].h, 16 * (it // 2 + 1))
                            ins = v.tensor_scalar(out=wub[it % 2][:], in0=wub[it % 2][:],
                                                  scalar1=m128c[:], scalar2=None, op0=ALU.add)
                            d_cvtu.inc(ins)
                            for tc in range(NTC):
                                idxp = it * 4 + tc
                                v.wait_ge(d_mm.h, 2 * idxp + 2)
                                v.wait_ge(d_sg.h, idxp + 1)
                                if idxp >= 2:
                                    v.wait_ge(d_hst[idxp % 2].h, 64 * (idxp // 2))
                                ins = v.scalar_tensor_tensor(
                                    out=hst[idxp % 2][:], in0=ps_u[idxp % 2][:],
                                    scalar=usc[:, it:it + 1], in1=sgb[idxp % 2][:],
                                    op0=ALU.mult, op1=ALU.mult)
                                d_h.inc(ins)

                    @blk.sync
                    def _(s):
                        for it in range(NIT):
                            for tc in range(NTC):
                                idxp = it * 4 + tc
                                s.wait_ge(d_h.h, idxp + 1)
                                for q in range(4):
                                    s.dma_start(
                                        out=h_sc[tc * 4 + q, it],
                                        in_=hst[idxp % 2][:, q * 128:(q + 1) * 128]
                                    ).then_inc(d_hst[idxp % 2].h, 16)
                                    d_hst[idxp % 2].n += 16

        if "E" in phases:
            with ExitStack() as pstack:
                psum = lambda name, shape, dt: pstack.enter_context(nc.psum_tensor(name, shape, dt))
                ps_e = [psum(f"ps_e{i}", [128, 128], F32) for i in range(3)]
                ps_y = [psum(f"ps_y{i}", [128, 128], F32) for i in range(3)]
                ps_s = [psum(f"ps_s{i}", [64, 64], F32) for i in range(2)]
                ps_btm = ps_y[0][:].bitcast(BF16)[:, 0:64]
                DDONE = d_mm.n
                DHST0 = d_hst[0].n; DHST1 = d_hst[1].n
                BEV = b_ev.n
                BZST0 = b_zst[0].n; BZST1 = b_zst[1].n

                with nc.Block() as blk:
                    @blk.vector
                    def _(v):
                        v.wait_ge(c_ld.h, CONST)
                        if "D" in phases:
                            v.wait_ge(d_mm.h, DDONE)
                            v.wait_ge(d_hst[0].h, DHST0)
                            v.wait_ge(d_hst[1].h, DHST1)
                        if "B" in phases:
                            v.wait_ge(b_ev.h, BEV)
                            v.wait_ge(b_bc.h, 16)
                            v.wait_ge(b_bcv.h, 1)
                        v.memset(S_f[:], 0.0)
                        ins = v.tensor_scalar(out=spx[:], in0=dtraw[:],
                                              scalar1=dtb[:], scalar2=None, op0=ALU.add)
                        e_pp.inc(ins)
                        v.wait_ge(e_pp.h, 3)
                        ins = v.tensor_tensor(out=dtT[:], in0=dtT[:], in1=alphaT[:],
                                              op=ALU.add)
                        e_pp.inc(ins)
                        ins = v.tensor_single_scalar(out=A_col[:], in_=a_tmp[:],
                                                     scalar=-1.0, op=ALU.mult)
                        e_pp.inc(ins)
                        v.wait_ge(e_pp.h, 5)
                        ins = v.tensor_scalar(out=alphaT[:], in0=dtT[:],
                                              scalar1=A_col[:], scalar2=None, op0=ALU.mult)
                        e_pp.inc(ins)
                        for c in range(NCH):
                            ins = v.tensor_tensor_scan(
                                out=lamT_all[:, c * 128:(c + 1) * 128],
                                data0=onesf[0:32, 0:128],
                                data1=alphaT[:, c * 128:(c + 1) * 128],
                                initial=0.0, op0=ALU.mult, op1=ALU.add)
                            e_scan.inc(ins)
                        for c in range(NCH):
                            v.wait_ge(e_vt.h, c + 1)
                            ins = v.tensor_tensor(out=W_allTp[c % 2][:],
                                                  in0=dtT[:, c * 128:(c + 1) * 128],
                                                  in1=V_Tp[c % 2][:], op=ALU.mult)
                            e_wt.inc(ins)
                            for k in range(4):
                                kk = 4 * c + k
                                v.wait_ge(e_pt.h, kk + 1)
                                if k == 0:
                                    ins = v.tensor_copy(out=Lam_all[:, c * 32:(c + 1) * 32],
                                                        in_=ps_e[kk % 3][:, 0:32])
                                elif k == 1:
                                    ins = v.tensor_copy(out=dt_tm_all[:, c * 32:(c + 1) * 32],
                                                        in_=ps_e[kk % 3][:, 0:32])
                                elif k == 2:
                                    ins = v.tensor_copy(out=W_all_all[:, c * 32:(c + 1) * 32],
                                                        in_=ps_e[kk % 3][:, 0:32])
                                else:
                                    ins = v.tensor_tensor(
                                        out=GT_all[:, c * 128:(c + 1) * 128],
                                        in0=ps_e[kk % 3][:], in1=tri[:], op=ALU.mult)
                                e_ptev.inc(ins)
                            v.wait_ge(e_btr.h, c + 1)
                            ins = v.tensor_copy(out=B_tm_all[:, c * 64:(c + 1) * 64],
                                                in_=ps_btm)
                            e_bev.inc(ins)
                        ins = v.tensor_single_scalar(out=negLam_all[:], in_=Lam_all[:],
                                                     scalar=-1.0, op=ALU.mult)
                        e_lev.inc(ins)
                        NG = 32 * NCH
                        def dve_tail(gidx):
                            c, h = gidx // 32, gidx % 32
                            par = c % 2
                            rows = slice(64 * (h % 2), 64 * (h % 2) + 64)
                            v.wait_ge(e_smm.h, gidx + 1)
                            ins = v.scalar_tensor_tensor(
                                out=S_f[:, 64 * h:64 * h + 64],
                                in0=S_f[:, 64 * h:64 * h + 64],
                                scalar=Dinb[gidx % 4][0:64, 127:128], in1=ps_s[gidx % 2][:],
                                op0=ALU.mult, op1=ALU.add)
                            e_sn.inc(ins)
                            v.wait_ge(e_sn.h, gidx + 1)
                            ins = v.tensor_copy(out=S_bf[:, 64 * h:64 * h + 64],
                                                in_=S_f[:, 64 * h:64 * h + 64])
                            e_sbf.inc(ins)
                            v.wait_ge(e_ymm.h, gidx + 1)
                            v.wait_ge(e_zs.h, c + 1)
                            if h == 0 and c >= 2:
                                v.wait_ge(e_ygst[c % 2].h, 16 * (c // 2))
                            ins = v.tensor_tensor(
                                out=ygb[par][rows, (h // 2) * 128:(h // 2) * 128 + 128],
                                in0=ps_y[gidx % 3][rows, :],
                                in1=zsl[par][rows, (h // 2) * 128:(h // 2) * 128 + 128],
                                op=ALU.mult)
                            e_yg.inc(ins)
                        def dve_d1(gidx):
                            c, h = gidx // 32, gidx % 32
                            v.wait_ge(e_bcx.h, gidx + 1)
                            if gidx >= 4:
                                v.wait_ge(e_exp.h, gidx - 3)
                            ins = v.scalar_tensor_tensor(
                                out=D1[gidx % 4][:], in0=ps_e[gidx % 3][:],
                                scalar=negLam_all[:, c * 32 + h:c * 32 + h + 1], in1=tri[:],
                                op0=ALU.add, op1=ALU.mult)
                            e_d1.inc(ins)
                        dve_d1(0)
                        for c in range(NCH):
                            for h in range(H):
                                gidx = 32 * c + h
                                ch = c * 32 + h
                                if gidx + 1 < NG:
                                    dve_d1(gidx + 1)
                                v.wait_ge(e_exp.h, gidx + 1)
                                if gidx >= 4:
                                    v.wait_ge(e_ymm.h, gidx - 3)
                                    v.wait_ge(e_smm.h, gidx - 3)
                                v.scalar_tensor_tensor(
                                    out=t1[gidx % 2][:], in0=E_sb[gidx % 4][:],
                                    scalar=dt_tm_all[:, ch:ch + 1],
                                    in1=GT_all[:, c * 128:(c + 1) * 128],
                                    op0=ALU.mult, op1=ALU.mult)
                                ins = v.scalar_tensor_tensor(
                                    out=Mf[gidx % 4][:], in0=identf[:],
                                    scalar=ddiag[:, h:h + 1], in1=t1[gidx % 2][:],
                                    op0=ALU.mult, op1=ALU.add)
                                e_mf.inc(ins)
                                v.wait_ge(e_din.h, gidx + 1)
                                ins = v.tensor_tensor(out=Cd[gidx % 4][:],
                                                      in0=CT_bf[:, c * 128:(c + 1) * 128],
                                                      in1=Dinb[gidx % 4][:], op=ALU.mult)
                                e_cd.inc(ins)
                                ins = v.tensor_scalar(out=Bd[gidx % 4][:],
                                                      in0=B_tm_all[:, c * 64:(c + 1) * 64],
                                                      scalar1=W_all_all[:, ch:ch + 1],
                                                      scalar2=None, op0=ALU.mult)
                                e_bd.inc(ins)
                                if gidx >= 2:
                                    dve_tail(gidx - 2)
                        dve_tail(NG - 2)
                        dve_tail(NG - 1)

                    @blk.scalar
                    def _(sc):
                        sc.wait_ge(c_ld.h, CONST)
                        if "D" in phases:
                            sc.wait_ge(d_mm.h, DDONE)
                        if "B" in phases:
                            sc.wait_ge(b_ev.h, BEV)
                        sc.wait_ge(e_pp.h, 1)
                        sc.activation(out=alphaT[:], in_=spx[:], func=AF.Abs)
                        sc.activation(out=alphaT[:], in_=alphaT[:], func=AF.Exp, scale=-1.0)
                        sc.activation(out=alphaT[:], in_=alphaT[:], func=AF.Ln, bias=1.0)
                        ins = sc.activation(out=dtT[:], in_=spx[:], func=AF.Relu)
                        e_pp.inc(ins)
                        ins = sc.activation(out=a_tmp[:], in_=alog[:], func=AF.Exp)
                        e_pp.inc(ins)
                        for c in range(NCH):
                            sc.wait_ge(e_scan.h, c + 1)
                            if c >= 2:
                                sc.wait_ge(e_wt.h, c - 1)
                            ins = sc.activation(out=V_Tp[c % 2][:],
                                                in_=lamT_all[:, c * 128:(c + 1) * 128],
                                                func=AF.Exp, scale=-1.0,
                                                bias=lamT_all[:, c * 128 + 127:c * 128 + 128])
                            e_vt.inc(ins)
                        sc.wait_ge(e_lev.h, 1)
                        for c in range(NCH):
                            par = c % 2
                            sc.wait_ge(e_zld[c % 2].h, 16 * (c // 2 + 1))
                            if c == 0:
                                sc.wait_ge(e_ptev.h, 64)
                                sc.wait_ge(e_wt.h, NCH)
                            if c >= 2:
                                sc.wait_ge(e_yg.h, 32 * (c - 1))
                            ins = sc.activation(out=zsl[par][:], in_=zch[par][:],
                                                func=AF.Silu)
                            e_zs.inc(ins)
                            for h in range(H):
                                hp = h % 2
                                gidx = 32 * c + h
                                sc.wait_ge(e_d1.h, gidx + 1)
                                if gidx >= 4:
                                    sc.wait_ge(e_mf.h, gidx - 3)
                                    sc.wait_ge(e_sn.h, gidx - 3)
                                ins = sc.activation(out=E_sb[gidx % 4][:], in_=D1[gidx % 4][:],
                                                    func=AF.Exp)
                                e_exp.inc(ins)
                                ins = sc.activation(out=Dinb[gidx % 4][:],
                                                    in_=ps_e[gidx % 3][0:64, :],
                                                    func=AF.Exp)
                                e_din.inc(ins)

                    @blk.tensor
                    def _(t):
                        if "B" in phases:
                            t.wait_ge(b_xev.h, b_xev.n)
                            t.wait_ge(b_bc.h, 16)
                            t.wait_ge(b_bcv.h, 1)
                        t.wait_ge(e_pp.h, 6)
                        for c in range(NCH):
                            t.wait_ge(e_scan.h, c + 1)
                            for k in range(4):
                                kk = 4 * c + k
                                if kk >= 3:
                                    t.wait_ge(e_ptev.h, kk - 2)
                                if k == 0:
                                    ins = t.transpose(ps_e[kk % 3][:, 0:32],
                                                      lamT_all[:, c * 128:(c + 1) * 128],
                                                      identf[0:32, 0:32])
                                elif k == 1:
                                    ins = t.transpose(ps_e[kk % 3][:, 0:32],
                                                      dtT[:, c * 128:(c + 1) * 128],
                                                      identf[0:32, 0:32])
                                elif k == 2:
                                    t.wait_ge(e_wt.h, c + 1)
                                    ins = t.transpose(ps_e[kk % 3][:, 0:32],
                                                      W_allTp[c % 2][:],
                                                      identf[0:32, 0:32])
                                else:
                                    ins = t.matmul(ps_e[kk % 3][:],
                                                   BT_bf[:, c * 128:(c + 1) * 128],
                                                   CT_bf[:, c * 128:(c + 1) * 128],
                                                   start=True, stop=True)
                                e_pt.inc(ins)
                            t.wait_ge(e_bev.h, c)
                            ins = t.transpose(ps_btm, BT_bf[:, c * 128:(c + 1) * 128],
                                              identb[0:64, 0:64])
                            e_btr.inc(ins)
                        NG = 32 * NCH
                        t.wait_ge(e_ptev.h, 64)
                        t.wait_ge(e_lev.h, 1)
                        def pe_bcast(gidx):
                            ch = (gidx // 32) * 32 + (gidx % 32)
                            if gidx >= 3:
                                t.wait_ge(e_d1.h, gidx - 2)
                                t.wait_ge(e_din.h, gidx - 2)
                            ins = t.transpose(
                                ps_e[gidx % 3][:],
                                Lam_all[:, ch:ch + 1].broadcast_to([128, 128]),
                                identf[:])
                            e_bcx.inc(ins)
                        def pe_mm3(gidx):
                            c, h = gidx // 32, gidx % 32
                            t.wait_ge(e_bd.h, gidx + 1)
                            if gidx >= 2:
                                t.wait_ge(e_sn.h, gidx - 1)
                            ins = t.matmul(ps_s[gidx % 2][:], Bd[gidx % 4][:],
                                           X_tm[:, c * D + 64 * h: c * D + 64 * h + 64],
                                           start=True, stop=True)
                            e_smm.inc(ins)
                        pe_bcast(0); pe_bcast(1)
                        for c in range(NCH):
                            for h in range(H):
                                hp = h % 2
                                gidx = 32 * c + h
                                if gidx + 2 < NG:
                                    pe_bcast(gidx + 2)
                                tp = (0, 64 * hp)
                                t.wait_ge(e_mf.h, gidx + 1)
                                if gidx >= 3:
                                    t.wait_ge(e_yg.h, gidx - 2)
                                rows = slice(64 * hp, 64 * hp + 64)
                                ins = t.matmul(ps_y[gidx % 3][rows, :],
                                               X_tm[:, c * D + 64 * h: c * D + 64 * h + 64],
                                               Mf[gidx % 4][:],
                                               start=True, stop=(c == 0),
                                               tile_position=tp)
                                if c > 0:
                                    t.wait_ge(e_cd.h, gidx + 1)
                                    t.wait_ge(e_sbf.h, 32 * (c - 1) + h + 1)
                                    ins = t.matmul(ps_y[gidx % 3][rows, :],
                                                   S_bf[:, 64 * h:64 * h + 64],
                                                   Cd[gidx % 4][:],
                                                   start=False, stop=True,
                                                   tile_position=tp)
                                e_ymm.inc(ins)
                                if gidx >= 1:
                                    pe_mm3(gidx - 1)
                        pe_mm3(NG - 1)

                    @blk.sync
                    def _(s):
                        if "D" in phases:
                            s.wait_ge(d_hst[0].h, DHST0)
                            s.wait_ge(d_hst[1].h, DHST1)
                            s.wait_ge(d_h.h, d_h.n)
                        if "B" in phases:
                            s.wait_ge(b_zst[0].h, BZST0)
                            s.wait_ge(b_zst[1].h, BZST1)
                        for c in range(NCH):
                            if c >= 2:
                                s.wait_ge(e_zs.h, c - 1)
                            s.dma_start(out=zch[c % 2].ap().rearrange("p (j t) -> p j t", j=16),
                                        in_=z_sc[c].rearrange("(j p) t -> p j t", p=128)
                                        ).then_inc(e_zld[c % 2].h, 16)
                            e_zld[c % 2].n += 16

                    @blk.gpsimd
                    def _(g):
                        for c in range(NCH):
                            g.wait_ge(e_yg.h, 32 * (c + 1))
                            g.dma_start(out=yg_sc[c].rearrange("(j p) t -> p j t", p=128),
                                        in_=ygb[c % 2].ap().rearrange("p (j t) -> p j t", j=16)
                                        ).then_inc(e_ygst[c % 2].h, 16)
                            e_ygst[c % 2].n += 16

        if "F" in phases:
            with ExitStack() as pstack:
                psum = lambda name, shape, dt: pstack.enter_context(nc.psum_tensor(name, shape, dt))
                ps_f = [psum(f"ps_f{i}", [128, 512], F32) for i in range(2)]
                EYG = e_yg.n; ESN = e_sn.n; EYMM = e_ymm.n; ESMM = e_smm.n
                EYGST0 = e_ygst[0].n; EYGST1 = e_ygst[1].n; ESBF = e_sbf.n
                with nc.Block() as blk:
                    @blk.gpsimd
                    def _(g):
                        if "E" in phases:
                            g.wait_ge(e_yg.h, EYG)
                            g.wait_ge(e_zs.h, NCH)
                            g.wait_ge(e_sn.h, ESN)
                            g.wait_ge(e_ymm.h, EYMM)
                            g.wait_ge(e_smm.h, ESMM)
                            g.wait_ge(e_ygst[0].h, EYGST0)
                            g.wait_ge(e_ygst[1].h, EYGST1)
                            g.wait_ge(e_sbf.h, ESBF)
                        for kt in range(NKT):
                            g.dma_start(out=WoutT[:, kt * T:(kt + 1) * T], in_=wout[kt]
                                        ).then_inc(f_wld.h, 16)
                            f_wld.n += 16

                    @blk.sync
                    def _(s):
                        if "E" in phases:
                            s.wait_ge(e_yg.h, e_yg.n)
                        for tt in range(NTT):
                            if "E" in phases:
                                s.wait_ge(e_ygst[tt % 2].h, 16 * (tt // 2 + 1))
                            if tt >= 2:
                                s.wait_ge(f_mm.h, 4 * (tt - 1))
                            s.dma_start(out=ygp[tt % 2].ap().rearrange("p (j t) -> p j t", j=16),
                                        in_=yg_sc[tt].rearrange("(j p) t -> p j t", p=128)
                                        ).then_inc(f_yld[tt % 2].h, 16)
                            f_yld[tt % 2].n += 16
                            for dc in range(4):
                                idx = 4 * tt + dc
                                s.wait_ge(f_sev.h, idx + 1)
                                s.dma_start(out=s_sc[tt][:, dc * 512:(dc + 1) * 512],
                                            in_=s_blk[idx % 2][:]
                                            ).then_inc(f_sst[idx % 2].h, 16)
                                f_sst[idx % 2].n += 16

                    @blk.tensor
                    def _(t):
                        t.wait_ge(f_wld.h, 16 * NKT)
                        for tt in range(NTT):
                            t.wait_ge(f_yld[tt % 2].h, 16 * (tt // 2 + 1))
                            for dc in range(4):
                                idx = 4 * tt + dc
                                if idx >= 2:
                                    t.wait_ge(f_sev.h, idx - 1)
                                for k in range(NKT):
                                    ins = t.matmul(
                                        ps_f[idx % 2][:],
                                        ygp[tt % 2][:, k * 128:(k + 1) * 128],
                                        WoutT[:, k * T + dc * 512: k * T + dc * 512 + 512],
                                        start=(k == 0), stop=(k == NKT - 1))
                                f_mm.inc(ins)

                    @blk.vector
                    def _(v):
                        for tt in range(NTT):
                            for dc in range(4):
                                idx = 4 * tt + dc
                                v.wait_ge(f_mm.h, idx + 1)
                                if idx >= 2:
                                    v.wait_ge(f_sst[idx % 2].h, 16 * (idx // 2))
                                ins = v.tensor_copy(out=s_blk[idx % 2][:],
                                                    in_=ps_f[idx % 2][:])
                                f_sev.inc(ins)

        if "G" in phases:
            with ExitStack() as pstack:
                psum = lambda name, shape, dt: pstack.enter_context(nc.psum_tensor(name, shape, dt))
                ps_dn = [psum(f"ps_dn{i}", [128, 512], F32) for i in range(8)]
                FMM = f_mm.n; FSST0 = f_sst[0].n; FSST1 = f_sst[1].n; DHST2a = d_hst[0].n; DHST2b = d_hst[1].n
                def gmidx(dc, ttg, half, ti):
                    return ((dc * 2 + ttg) * 2 + half) * 8 + ti + 1
                def silidx(dc, ttg, ti):
                    return (dc * 2 + ttg) * 8 + ti + 1
                with nc.Block() as blk:
                    @blk.gpsimd
                    def _(g):
                        if "F" in phases:
                            g.wait_ge(f_mm.h, FMM)
                        for ci in range(8):
                            dc, half = ci // 2, ci % 2
                            if ci >= 2:
                                g.wait_ge(g_mm.h, gmidx(dc - 1, 1, half, 7))
                            for j in range(32):
                                g.dma_start(out=wdp[ci % 2][:, j * 512:(j + 1) * 512],
                                            in_=wd[dc, half, j]).then_inc(g_wld[ci % 2].h, 16)
                                g_wld[ci % 2].n += 16


                    @blk.sync
                    def _(s):
                        if "D" in phases:
                            s.wait_ge(d_hst[0].h, DHST2a)
                            s.wait_ge(d_hst[1].h, DHST2b)
                        if "F" in phases:
                            s.wait_ge(f_sst[0].h, FSST0)
                            s.wait_ge(f_sst[1].h, FSST1)
                        for dc in range(4):
                            if dc >= 2:
                                s.wait_ge(g_scl.h, silidx(dc - 2, 1, 7))
                            s.dma_start(out=dscst[dc % 2][:], in_=dsc_d[dc]
                                        ).then_inc(g_dsc[dc % 2].h, 16)
                            g_dsc[dc % 2].n += 16
                            for ttg in range(2):
                                for half in range(2):
                                    for ti in range(8):
                                        tt = ttg * 8 + ti
                                        gi = gmidx(dc, ttg, half, ti)
                                        hb = (gi - 1) % 2
                                        if gi > 2:
                                            s.wait_ge(g_mm.h, gi - 2)
                                        s.dma_start(
                                            out=h_tt[hb].ap().rearrange("p (i t) -> p i t", i=32),
                                            in_=h_sc[tt, half * 32:(half + 1) * 32].rearrange(
                                                "i p t -> p i t")
                                        ).then_inc(g_hld[(gi - 1) % 2].h, 16)
                                        g_hld[(gi - 1) % 2].n += 16


                    @blk.tensor
                    def _(t):
                        for dc in range(4):
                            for ttg in range(2):
                                for half in range(2):
                                    ci = 2 * dc + half
                                    t.wait_ge(g_cvt.h, 32 * (ci + 1))
                                    for ti in range(8):
                                        gi = gmidx(dc, ttg, half, ti)
                                        hb = (gi - 1) % 2
                                        t.wait_ge(g_hld[(gi - 1) % 2].h, 16 * ((gi - 1) // 2 + 1))
                                        if half == 0:
                                            si_prev = silidx(dc, ttg, ti) - 8
                                            if si_prev > 0:
                                                t.wait_ge(g_scl.h, si_prev)
                                        for j in range(32):
                                            ins = t.matmul(
                                                ps_dn[ti][:],
                                                h_tt[hb][:, j * 128:(j + 1) * 128],
                                                wdp[ci % 2][:, j * 512:(j + 1) * 512],
                                                start=(half == 0 and j == 0),
                                                stop=(half == 1 and j == 31))
                                        g_mm.inc(ins)

                    @blk.scalar
                    def _(sc):
                        sc.wait_ge(c_ld.h, CONST)
                        def cvt(ci):
                            dc2, half2 = ci // 2, ci % 2
                            if ci >= 2:
                                sc.wait_ge(g_mm.h, gmidx(dc2 - 1, 1, half2, 7))
                            sc.wait_ge(g_wld[ci % 2].h, 512 * (ci // 2 + 1))
                            for j in range(32):
                                ins = sc.activation(out=wdp[ci % 2][:, j * 512:(j + 1) * 512],
                                                    in_=wdp[ci % 2][:, j * 512:(j + 1) * 512],
                                                    func=AF.Identity, bias=m128c[:])
                                g_cvt.inc(ins)
                        cvt(0); cvt(1)
                        for dc in range(4):
                            for ttg in range(2):
                                for ti in range(8):
                                    tt = ttg * 8 + ti
                                    si = silidx(dc, ttg, ti)
                                    if si > 2:
                                        sc.wait_ge(g_ob.h, si - 2)
                                    sc.dma_start(out=sstrip[ti % 2][:],
                                                 in_=s_sc[tt][:, dc * 512:(dc + 1) * 512]
                                                 ).then_inc(g_sld[(si - 1) % 2].h, 16)
                                    g_sld[(si - 1) % 2].n += 16
                                    sc.dma_start(out=ustrip[ti % 2][:],
                                                 in_=u[tt * 128:(tt + 1) * 128,
                                                      dc * 512:(dc + 1) * 512]
                                                 ).then_inc(g_uld[(si - 1) % 2].h, 16)
                                    g_uld[(si - 1) % 2].n += 16
                                    sc.wait_ge(g_mm.h, gmidx(dc, ttg, 1, ti))
                                    sc.wait_ge(g_scl.h, si)
                                    ins = sc.activation(out=gsil[ti % 2][:],
                                                        in_=scld[ti % 2][:], func=AF.Silu)
                                    g_sil.inc(ins)
                                    sc.wait_ge(g_ob.h, si)
                                    sc.dma_start(out=y[tt * 128:(tt + 1) * 128,
                                                       dc * 512:(dc + 1) * 512],
                                                 in_=outb[ti % 2][:]
                                                 ).then_inc(g_ost[(si - 1) % 2].h, 16)
                                    g_ost[(si - 1) % 2].n += 16
                                if dc * 2 + ttg + 2 <= 7:
                                    cvt(dc * 2 + ttg + 2)
                        sc.wait_ge(g_ost[0].h, g_ost[0].n)
                        sc.wait_ge(g_ost[1].h, g_ost[1].n)

                    @blk.vector
                    def _(v):
                        for dc in range(4):
                            for ttg in range(2):
                                for ti in range(8):
                                    si = silidx(dc, ttg, ti)
                                    v.wait_ge(g_mm.h, gmidx(dc, ttg, 1, ti))
                                    v.wait_ge(g_dsc[dc % 2].h, 16 * (dc // 2 + 1))
                                    if si > 2:
                                        v.wait_ge(g_sil.h, si - 2)
                                    ins = v.tensor_tensor(out=scld[ti % 2][:],
                                                          in0=ps_dn[ti][:],
                                                          in1=dscst[dc % 2][:], op=ALU.mult)
                                    g_scl.inc(ins)
                                    v.wait_ge(g_sil.h, si)
                                    v.wait_ge(g_sld[(si - 1) % 2].h, 16 * ((si - 1) // 2 + 1))
                                    v.wait_ge(g_uld[(si - 1) % 2].h, 16 * ((si - 1) // 2 + 1))
                                    if si > 2:
                                        v.wait_ge(g_ost[(si - 1) % 2].h, 16 * ((si - 1) // 2))
                                    v.tensor_tensor(out=outb[ti % 2][:], in0=gsil[ti % 2][:],
                                                    in1=sstrip[ti % 2][:], op=ALU.mult)
                                    ins = v.tensor_tensor(out=outb[ti % 2][:],
                                                          in0=outb[ti % 2][:],
                                                          in1=ustrip[ti % 2][:], op=ALU.add)
                                    g_ob.inc(ins)

        if taps:
            tap_src = {
                "xnT": (lambda: xnT[:], lambda s: s.wait_ge(a_ev.h, a_ev.n)),
                "X_tm": (lambda: X_tm[:], lambda s: s.wait_ge(b_xev.h, b_xev.n)),
                "BT_bf": (lambda: BT_bf[:], lambda s: (s.wait_ge(b_bc.h, b_bc.n), s.wait_ge(b_bcv.h, b_bcv.n))),
                "CT_bf": (lambda: CT_bf[:], lambda s: (s.wait_ge(b_bc.h, b_bc.n), s.wait_ge(b_bcv.h, b_bcv.n))),
                "dtraw": (lambda: dtraw[:], lambda s: s.wait_ge(b_ev.h, b_ev.n)),
                "z_sc": (lambda: z_sc.rearrange("c d t -> (c d) t"),
                         lambda s: (s.wait_ge(b_zst[0].h, b_zst[0].n), s.wait_ge(b_zst[1].h, b_zst[1].n))),
                "yg_sc": (lambda: yg_sc.rearrange("c d t -> (c d) t"),
                          lambda s: (s.wait_ge(e_ygst[0].h, e_ygst[0].n), s.wait_ge(e_ygst[1].h, e_ygst[1].n))),
                "h_sc": (lambda: h_sc.rearrange("a b p t -> (a b p) t"),
                         lambda s: (s.wait_ge(d_hst[0].h, d_hst[0].n), s.wait_ge(d_hst[1].h, d_hst[1].n))),
                "s_sc": (lambda: s_sc.rearrange("a p t -> (a p) t"),
                         lambda s: (s.wait_ge(f_sst[0].h, f_sst[0].n), s.wait_ge(f_sst[1].h, f_sst[1].n))),
            }
            with nc.Block() as blk:
                @blk.sync
                def _(s):
                    done = sem("tapdone")
                    for tname, shape, dt in taps:
                        src, wait = tap_src[tname]
                        if wait is not None:
                            wait(s)
                        s.dma_start(out=taps_out[tname], in_=src()).then_inc(done.h, 16)
                        done.n += 16
                    s.wait_ge(done.h, done.n)

    return nc



def _prep_common(L0_attn_W_in, L0_attn_conv_w, L0_attn_conv_b, L0_attn_W_out,
                 L0_attn_A_log, L0_attn_D_param, L0_attn_dt_bias,
                 L0_ffn_gate_w, L0_ffn_gate_scale, L0_ffn_up_w, L0_ffn_up_scale,
                 L0_ffn_down_w, L0_ffn_down_scale):
    f = np.float32
    win_p = np.zeros((D, NJT * 128), f)
    win_p[:, :JPROJ] = np.asarray(L0_attn_W_in, f).T
    win_t = np.ascontiguousarray(
        win_p.reshape(NKT, 128, NJT, 128).transpose(2, 1, 0, 3).reshape(NJT, 128, NKT * 128))
    woutT = np.asarray(L0_attn_W_out, f).T
    wout_t = np.ascontiguousarray(woutT.reshape(NKT, 128, D))
    wgT = np.asarray(L0_ffn_gate_w, np.int32).T
    wg_t = np.ascontiguousarray(
        wgT.reshape(NKT, 128, NIT, 128).transpose(2, 1, 0, 3).reshape(NIT, 128, NKT * 128))
    wuT = np.asarray(L0_ffn_up_w, np.int32).T
    wu_t = np.ascontiguousarray(
        wuT.reshape(NKT, 128, NIT, 128).transpose(2, 1, 0, 3).reshape(NIT, 128, NKT * 128))
    wdT = np.asarray(L0_ffn_down_w, np.int32).T
    wd_t = np.ascontiguousarray(
        wdT.reshape(2, 32, 128, 4, 512).transpose(3, 0, 1, 2, 4))
    gsc = np.ascontiguousarray(np.asarray(L0_ffn_gate_scale, f).reshape(NIT, 128).T)
    usc = np.ascontiguousarray(np.asarray(L0_ffn_up_scale, f).reshape(NIT, 128).T)
    dsc = np.ascontiguousarray(
        np.broadcast_to(np.asarray(L0_ffn_down_scale, f).reshape(4, 1, 512), (4, 128, 512)))
    cw = np.ascontiguousarray(np.asarray(L0_attn_conv_w, f)[:, 0, :].reshape(17, 128, 4))
    cb = np.ascontiguousarray(np.asarray(L0_attn_conv_b, f).reshape(17, 128, 1))
    dtbv = np.ascontiguousarray(np.asarray(L0_attn_dt_bias, f).reshape(32, 1))
    alogv = np.ascontiguousarray(np.asarray(L0_attn_A_log, f).reshape(32, 1))
    ddiag = np.ascontiguousarray(np.tile(np.asarray(L0_attn_D_param, f), (128, 1)))
    triv = np.triu(np.ones((128, 128), f))
    idf = np.eye(128, dtype=f)
    eps = np.full((128, 1), 1e-5, f)
    ones = np.ones((128, 128), f)
    return dict(win=win_t, wout=wout_t, wg=wg_t, wu=wu_t, wd=wd_t,
                gsc=gsc, usc=usc, dsc=dsc, cw=cw, cb=cb, dtb=dtbv, alog=alogv,
                ddiag=ddiag, tri=triv, idf=idf, eps=eps, onesd=ones,
                mnegd=np.ascontiguousarray((np.tril(np.ones((128, 128), f), -1) * -1e30).T),
                m128d=np.full((128, 1), -128.0, f))


def kernel(**inputs):
    common = _prep_common(**{k: v for k, v in inputs.items() if k != "L0_attn_u"})
    u_all = np.ascontiguousarray(np.asarray(inputs["L0_attn_u"], np.float32))
    nc = build()
    in_maps = [dict(common, u=u_all[b]) for b in range(8)]
    res = run_bass_kernel_spmd(nc, in_maps, core_ids=list(range(8)))
    return np.stack([r["y"] for r in res.results]).astype(np.float32)
